# version 19
# baseline (speedup 1.0000x reference)
import os
import numpy as np
import ml_dtypes
import concourse.bass as bass
import concourse.mybir as mybir
from concourse.bass_utils import run_bass_kernel_spmd
from contextlib import ExitStack

F32 = mybir.dt.float32
BF16 = mybir.dt.bfloat16
AF = mybir.ActivationFunctionType
ALU = mybir.AluOpType
AX = mybir.AxisListType

P = 128
NT = 16
D = 1024
DC = 8
FF = 2048
FC = 16
NE = 8
LN_EPS = 1e-5

SUM, VS, NEGM, V, STD1, STD0, RS1, RS0, NM1, NM0, R2, SPARE = range(12)


class Sem:
    def __init__(self, h):
        self.h = h
        self.n = 0

    def inc(self, k=1):
        self.n += k
        return self.n

    def dma(self):
        self.n += 16
        return self.n


def build_program(cap="full"):
    nc = bass.Bass("TRN2", target_bir_lowering=False, detect_race_conditions=False)

    dri = lambda n, s, d=F32: nc.dram_tensor(n, s, d, kind="ExternalInput").ap()
    x_own = dri("x_own", [NT, P, D])
    x_carry = dri("x_carry", [NT, P, D])
    flag_d = dri("flag_d", [P, 1])
    wq_d = dri("wq_d", [24, P, DC, P])
    wg_d = dri("wg_d", [NE, FC, P, DC, P])
    wu_d = dri("wu_d", [NE, FC, P, DC, P])
    wd_d = dri("wd_d", [NE, P, FC, D])
    wr_d = dri("wr_d", [P, DC, NE])
    gc_d = dri("gc_d", [P, DC])
    lng_d = dri("lng_d", [P, D])
    lnb_d = dri("lnb_d", [P, D])
    id_d = dri("id_d", [P, P], BF16)
    out_d = nc.dram_tensor("out_d", [NT, P, D], F32, kind="ExternalOutput").ap()
    dbg = os.environ.get("KDBG", "0") == "1"
    if dbg:
        dbg_o = {
            "xfm_o": nc.dram_tensor("xfm_o", [P, DC, 2048], BF16, kind="ExternalOutput").ap(),
            "bufA_o": nc.dram_tensor("bufA_o", [P, DC, 2048], BF16, kind="ExternalOutput").ap(),
            "stm_o": nc.dram_tensor("stm_o", [P, NT, D], BF16, kind="ExternalOutput").ap(),
            "acc_o": nc.dram_tensor("acc_o", [P, 8, D], F32, kind="ExternalOutput").ap(),
            "w8_o": nc.dram_tensor("w8_o", [P, NT, NE], F32, kind="ExternalOutput").ap(),
            "cols_o": nc.dram_tensor("cols_o", [P, NT, 12], F32, kind="ExternalOutput").ap(),
            "ab_o": nc.dram_tensor("ab_o", [P, 2048], F32, kind="ExternalOutput").ap(),
            "yb_o": nc.dram_tensor("yb_o", [P, 2048], BF16, kind="ExternalOutput").ap(),
            "kvb_o": nc.dram_tensor("kvb_o", [P, 2048], BF16, kind="ExternalOutput").ap(),
            "qb_o": nc.dram_tensor("qb_o", [P, 2048], BF16, kind="ExternalOutput").ap(),
        }
    else:
        dbg_o = None

    ctx = ExitStack()
    sb = lambda n, s, d: ctx.enter_context(nc.sbuf_tensor(n, s, d))

    xfm = sb("xfm", [P, DC, 2048], BF16)
    bufA = sb("bufA", [P, DC, 2048], BF16)
    s_tm = sb("s_tm", [P, NT, D], BF16)
    acc = sb("acc", [P, 8, D], F32)
    wd_sb = sb("wd_sb", [P, FC, D], BF16)
    wring = sb("wring", [P, 5, DC, P], BF16)
    kvb = sb("kvb", [P, 2048], BF16)
    qb = sb("qb", [P, 2048], BF16)
    ab = sb("ab", [P, 2048], F32)
    yb = sb("yb", [P, 2048], BF16)
    gtmp = sb("gtmp", [P, D], F32)
    xtmb = sb("xtmb", [P, 2, D], BF16)
    lng = sb("lng", [P, D], BF16)
    lnb = sb("lnb", [P, D], BF16)
    wrsb = sb("wrsb", [P, DC, NE], BF16)
    w8 = sb("w8", [P, NT, NE], F32)
    ident = sb("ident", [P, P], BF16)
    gcols = sb("gcols", [P, DC], F32)
    flag = sb("flag", [P, 1], F32)
    colbuf = sb("colbuf", [P, NT, 12], F32)
    epsc = sb("epsc", [P, 1], F32)
    ics = [sb(f"ic{c}", [P, 1], F32) for c in range(DC)]

    pmm = ctx.enter_context(nc.psum_tensor("pmm", [P, 6, 512], F32))
    ptr = ctx.enter_context(nc.psum_tensor("ptr", [P, 2, DC, P], BF16))

    sem = lambda n: Sem(ctx.enter_context(nc.semaphore(n)))
    d_cst = sem("d_cst")
    d_cstg = sem("d_cstg")
    d_x = [sem("d_x0"), sem("d_x1")]
    f_x = sem("f_x")
    d_wq = [sem(f"d_wq{i}") for i in range(5)]
    f_wq = sem("f_wq")
    d_wd = sem("d_wd")
    f_wd = sem("f_wd")
    d_xa = sem("d_xa")
    d_out = sem("d_out")
    t2v = sem("t2v")
    e_ps = sem("e_ps")
    s2v = sem("s2v")
    v2s = sem("v2s")
    stg = sem("stg")
    v2g = sem("v2g")

    d_dbg = sem("d_dbg")

    def _dbgpack():
        if not dbg:
            return None
        srcs = {"xfm_o": xfm[:], "bufA_o": bufA[:], "stm_o": s_tm[:],
                "acc_o": acc[:], "w8_o": w8[:], "cols_o": colbuf[:],
                "ab_o": ab[:], "yb_o": yb[:], "kvb_o": kvb[:], "qb_o": qb[:]}
        return (dbg_o, srcs, d_dbg)

    plan = {k: [] for k in ("sync", "gpsimd", "tensor", "vector", "scalar")}

    def em(eng, fn):
        plan[eng].append(fn)

    def wait(eng, s, v):
        if v > 0:
            em(eng, lambda e, s=s, v=v: e.wait_ge(s.h, v))

    slot_rel = [None] * 6
    ptr_rel = [None] * 2
    slot_ctr = [0]

    def acquire(n):
        out = []
        for _ in range(n):
            s = slot_ctr[0] % 6
            slot_ctr[0] += 1
            if slot_rel[s] is not None:
                wait("tensor", slot_rel[s][0], slot_rel[s][1])
            out.append(s)
        return out

    em("sync", lambda e: e.dma_start(out=ident[:], in_=id_d).then_inc(d_cst.h, 16)); d_cst.dma()
    em("sync", lambda e: e.dma_start(out=gcols[:], in_=gc_d).then_inc(d_cst.h, 16)); d_cst.dma()
    em("sync", lambda e: e.dma_start(out=flag[:], in_=flag_d).then_inc(d_cst.h, 16)); d_cst.dma()
    CST_ALL = d_cst.n
    em("gpsimd", lambda e: e.dma_start(out=lng[:], in_=lng_d).then_inc(d_cstg.h, 16)); d_cstg.dma()
    em("gpsimd", lambda e: e.dma_start(out=lnb[:], in_=lnb_d).then_inc(d_cstg.h, 16)); d_cstg.dma()
    em("gpsimd", lambda e: e.dma_start(out=wrsb[:], in_=wr_d).then_inc(d_cstg.h, 16)); d_cstg.dma()
    CSTG_ALL = d_cstg.n
    wait("tensor", d_cst, CST_ALL)
    wait("vector", d_cst, CST_ALL)
    wait("scalar", d_cst, CST_ALL)
    em("vector", lambda e: e.memset(epsc[:], LN_EPS).then_inc(v2s.h, 1))
    v2s.inc()

    x_seq = [0]
    xfree = []

    def x_dma(src_ap):
        i = x_seq[0]
        x_seq[0] += 1
        slot = i % 2
        if i >= 2:
            for sm, vv in xfree[i - 2]:
                wait("sync", sm, vv)
        em("sync", lambda e, s=slot, a=src_ap: e.dma_start(out=acc[:, 1 + s, :], in_=a)
           .then_inc(d_x[s].h, 16))
        return slot, d_x[slot].dma()

    def tile_stats(i, slot, wv, xn_also):
        wait("scalar", d_x[slot], wv)
        em("scalar", lambda e, i=i, s=slot: e.activation(
            acc[:, 0, :], acc[:, 1 + s, :], AF.Identity,
            accum_out=colbuf[:, i, SUM:SUM + 1]).then_inc(s2v.h, 1))
        s2v.inc()
        wait("vector", s2v, s2v.n)
        em("vector", lambda e, i=i: e.tensor_scalar(
            colbuf[:, i, NEGM:NEGM + 1], colbuf[:, i, SUM:SUM + 1], -1.0 / D, None,
            ALU.mult).then_inc(v2s.h, 1))
        v2s.inc()
        wait("scalar", v2s, v2s.n)
        em("scalar", lambda e, i=i, s=slot: e.activation(
            acc[:, 0, :], acc[:, 1 + s, :], AF.Square, bias=colbuf[:, i, NEGM:NEGM + 1],
            accum_out=colbuf[:, i, VS:VS + 1]).then_inc(s2v.h, 1))
        s2v.inc()
        wait("vector", s2v, s2v.n)
        em("vector", lambda e, i=i: e.tensor_scalar(
            colbuf[:, i, V:V + 1], colbuf[:, i, VS:VS + 1], 1.0 / D, None,
            ALU.mult).then_inc(v2s.h, 1))
        v2s.inc()
        wait("scalar", v2s, v2s.n)
        em("scalar", lambda e, i=i: e.activation(
            colbuf[:, i, STD1:STD1 + 1], colbuf[:, i, V:V + 1], AF.Sqrt, bias=epsc[:, 0:1]))
        em("scalar", lambda e, i=i: e.activation(
            colbuf[:, i, STD0:STD0 + 1], colbuf[:, i, V:V + 1], AF.Sqrt).then_inc(s2v.h, 1))
        s2v.inc()
        wait("vector", s2v, s2v.n)
        em("vector", lambda e, i=i: e.reciprocal(colbuf[:, i, RS1:RS1 + 1], colbuf[:, i, STD1:STD1 + 1]))
        em("vector", lambda e, i=i: e.reciprocal(colbuf[:, i, RS0:RS0 + 1], colbuf[:, i, STD0:STD0 + 1]))
        em("vector", lambda e, i=i: e.tensor_scalar(
            colbuf[:, i, SPARE:SPARE + 1], colbuf[:, i, SUM:SUM + 1], 0.0, None, ALU.mult))
        if xn_also:
            em("vector", lambda e, i=i: e.tensor_tensor(
                colbuf[:, i, NM0:NM0 + 1], colbuf[:, i, NEGM:NEGM + 1], colbuf[:, i, RS0:RS0 + 1], ALU.mult))
            em("vector", lambda e, i=i: e.tensor_tensor(
                colbuf[:, i, NM1:NM1 + 1], colbuf[:, i, NEGM:NEGM + 1], colbuf[:, i, RS1:RS1 + 1], ALU.mult))
            em("vector", lambda e, i=i: e.tensor_tensor(
                colbuf[:, i, R2:R2 + 1], colbuf[:, i, RS0:RS0 + 1],
                colbuf[:, i, STD1:STD1 + 1], ALU.mult).then_inc(v2s.h, 1))
        else:
            em("vector", lambda e, i=i: e.tensor_tensor(
                colbuf[:, i, NM0:NM0 + 1], colbuf[:, i, NEGM:NEGM + 1],
                colbuf[:, i, RS0:RS0 + 1], ALU.mult).then_inc(v2s.h, 1))
        v2s.inc()
        wait("scalar", v2s, v2s.n)
        if xn_also:
            em("scalar", lambda e, i=i, s=slot: e.activation(
                xtmb[:, 0, :], acc[:, 1 + s, :], AF.Identity,
                bias=colbuf[:, i, NM1:NM1 + 1], scale=colbuf[:, i, RS1:RS1 + 1]).then_inc(s2v.h, 1))
            xfree.append((s2v, s2v.inc()))
            wait("vector", s2v, s2v.n)
            em("vector", lambda e, i=i: e.tensor_scalar(
                xtmb[:, 1, :], xtmb[:, 0, :], colbuf[:, i, R2:R2 + 1], None, ALU.mult)
               .then_inc(stg.h, 1))
            stg.inc()
        else:
            em("scalar", lambda e, i=i, s=slot: e.activation(
                xtmb[:, 1, :], acc[:, 1 + s, :], AF.Identity,
                bias=colbuf[:, i, NM0:NM0 + 1], scale=colbuf[:, i, RS0:RS0 + 1]).then_inc(stg.h, 1))
            xfree.append([(stg, stg.inc())])

    def transpose_pack(bank, src_slices, n):
        if ptr_rel[bank] is not None:
            wait("tensor", ptr_rel[bank][0], ptr_rel[bank][1])
        for j in range(n):
            last = j == n - 1
            em("tensor", lambda e, b=bank, j=j, sl=src_slices[j], l=last:
               (e.transpose(ptr[:, b, j, :], sl, ident[:]).then_inc(t2v.h, 1)
                if l else e.transpose(ptr[:, b, j, :], sl, ident[:])))
        return t2v.inc()

    if cap == "p0":
        return _finish(nc, ctx, plan, em, wait, d_out, stg, e_ps, _dbgpack())
    SQ, M, MSQ = VS, NEGM, SPARE

    def stats_region(x_src, xn_also):
        for i in range(NT):
            slot, wv = x_dma(x_src[i])
            wait("vector", d_x[slot], wv)
            em("vector", lambda e, i=i, s=slot: e.reduce_sum(
                colbuf[:, i, SUM:SUM + 1], acc[:, 1 + s, :], AX.X).then_inc(v2s.h, 1))
            rsv = v2s.inc()
            wait("scalar", d_x[slot], wv)
            em("scalar", lambda e, i=i, s=slot: e.activation(
                acc[:, 0, :], acc[:, 1 + s, :], AF.Square,
                accum_out=colbuf[:, i, SQ:SQ + 1]).then_inc(s2v.h, 1))
            xfree.append([(s2v, s2v.inc()), (v2s, rsv)])
        wait("vector", s2v, s2v.n)
        for i in range(NT):
            em("vector", lambda e, i=i: e.tensor_scalar(
                colbuf[:, i, M:M + 1], colbuf[:, i, SUM:SUM + 1], 1.0 / D, None, ALU.mult))
        for i in range(NT):
            em("vector", lambda e, i=i: e.tensor_tensor(
                colbuf[:, i, MSQ:MSQ + 1], colbuf[:, i, M:M + 1], colbuf[:, i, M:M + 1],
                ALU.mult))
        for i in range(NT):
            last = i == NT - 1
            em("vector", lambda e, i=i, l=last:
               (e.scalar_tensor_tensor(
                   out=colbuf[:, i, V:V + 1], in0=colbuf[:, i, SQ:SQ + 1], scalar=1.0 / D,
                   in1=colbuf[:, i, MSQ:MSQ + 1], op0=ALU.mult,
                   op1=ALU.subtract).then_inc(v2s.h, 1) if l else
                e.scalar_tensor_tensor(
                   out=colbuf[:, i, V:V + 1], in0=colbuf[:, i, SQ:SQ + 1], scalar=1.0 / D,
                   in1=colbuf[:, i, MSQ:MSQ + 1], op0=ALU.mult, op1=ALU.subtract)))
        wait("scalar", v2s, v2s.inc())
        for i in range(NT):
            em("scalar", lambda e, i=i: e.activation(
                colbuf[:, i, STD1:STD1 + 1], colbuf[:, i, V:V + 1], AF.Sqrt,
                bias=epsc[:, 0:1]))
        for i in range(NT):
            last = i == NT - 1
            em("scalar", lambda e, i=i, l=last:
               (e.activation(colbuf[:, i, STD0:STD0 + 1], colbuf[:, i, V:V + 1],
                             AF.Sqrt).then_inc(s2v.h, 1) if l else
                e.activation(colbuf[:, i, STD0:STD0 + 1], colbuf[:, i, V:V + 1], AF.Sqrt)))
        wait("vector", s2v, s2v.inc())
        for i in range(NT):
            em("vector", lambda e, i=i: e.reciprocal(
                colbuf[:, i, RS1:RS1 + 1], colbuf[:, i, STD1:STD1 + 1]))
        for i in range(NT):
            em("vector", lambda e, i=i: e.reciprocal(
                colbuf[:, i, RS0:RS0 + 1], colbuf[:, i, STD0:STD0 + 1]))
        for i in range(NT):
            em("vector", lambda e, i=i: e.tensor_tensor(
                colbuf[:, i, NM0:NM0 + 1], colbuf[:, i, M:M + 1], colbuf[:, i, RS0:RS0 + 1],
                ALU.mult))
        if xn_also:
            for i in range(NT):
                em("vector", lambda e, i=i: e.tensor_tensor(
                    colbuf[:, i, NM1:NM1 + 1], colbuf[:, i, M:M + 1],
                    colbuf[:, i, RS1:RS1 + 1], ALU.mult))
            for i in range(NT):
                em("vector", lambda e, i=i: e.tensor_scalar(
                    colbuf[:, i, NM1:NM1 + 1], colbuf[:, i, NM1:NM1 + 1], -1.0, None,
                    ALU.mult))
            for i in range(NT):
                em("vector", lambda e, i=i: e.tensor_scalar(
                    colbuf[:, i, NM0:NM0 + 1], colbuf[:, i, NM0:NM0 + 1], -1.0, None,
                    ALU.mult))
            for i in range(NT):
                last = i == NT - 1
                em("vector", lambda e, i=i, l=last:
                   (e.tensor_tensor(colbuf[:, i, R2:R2 + 1], colbuf[:, i, RS0:RS0 + 1],
                                    colbuf[:, i, STD1:STD1 + 1], ALU.mult)
                    .then_inc(v2s.h, 1) if l else
                    e.tensor_tensor(colbuf[:, i, R2:R2 + 1], colbuf[:, i, RS0:RS0 + 1],
                                    colbuf[:, i, STD1:STD1 + 1], ALU.mult)))
        else:
            for i in range(NT):
                last = i == NT - 1
                em("vector", lambda e, i=i, l=last:
                   (e.tensor_scalar(colbuf[:, i, NM0:NM0 + 1], colbuf[:, i, NM0:NM0 + 1],
                                    -1.0, None, ALU.mult).then_inc(v2s.h, 1) if l else
                    e.tensor_scalar(colbuf[:, i, NM0:NM0 + 1], colbuf[:, i, NM0:NM0 + 1],
                                    -1.0, None, ALU.mult)))
        wait("scalar", v2s, v2s.inc())

    stats_region(x_carry, xn_also=False)
    for i in range(NT):
        slot, wv = x_dma(x_carry[i])
        wait("scalar", d_x[slot], wv)
        em("scalar", lambda e, i=i, s=slot: e.activation(
            xtmb[:, 1, :], acc[:, 1 + s, :], AF.Identity,
            bias=colbuf[:, i, NM0:NM0 + 1], scale=colbuf[:, i, RS0:RS0 + 1])
           .then_inc(stg.h, 1))
        xfree.append([(stg, stg.inc())])
        wait("tensor", stg, stg.n)
        bank = i % 2
        tv = transpose_pack(bank, [xtmb[:, 1, c * P:(c + 1) * P] for c in range(DC)], DC)
        wait("scalar", t2v, tv)
        wait("vector", t2v, tv)
        for c in range(DC):
            em("vector", lambda e, i=i, b=bank, c=c: e.tensor_scalar(
                s_tm[:, 2 * c + i // 8, (i % 8) * P:(i % 8 + 1) * P], ptr[:, b, c, :],
                gcols[:, c:c + 1], None, ALU.mult).then_inc(e_ps.h, 1))
            e_ps.inc()
        ptr_rel[bank] = (e_ps, e_ps.n)
    P1C_VEC = e_ps.n
    if cap == "p1c":
        return _finish(nc, ctx, plan, em, wait, d_out, stg, e_ps, _dbgpack())

    stats_region(x_own, xn_also=True)
    for i in range(NT):
        slot, wv = x_dma(x_own[i])
        wait("scalar", d_x[slot], wv)
        em("scalar", lambda e, i=i, s=slot: e.activation(
            xtmb[:, 0, :], acc[:, 1 + s, :], AF.Identity,
            bias=colbuf[:, i, NM1:NM1 + 1], scale=colbuf[:, i, RS1:RS1 + 1])
           .then_inc(s2v.h, 1))
        xfree.append([(s2v, s2v.inc())])
        wait("vector", s2v, s2v.n)
        em("vector", lambda e, i=i: e.tensor_scalar(
            xtmb[:, 1, :], xtmb[:, 0, :], colbuf[:, i, R2:R2 + 1], None, ALU.mult)
           .then_inc(stg.h, 1))
        stg.inc()
        wait("tensor", stg, stg.n)
        bank = i % 2
        tv = transpose_pack(bank, [xtmb[:, 0, c * P:(c + 1) * P] for c in range(DC)], DC)
        wait("vector", t2v, tv)
        em("vector", lambda e, i=i, b=bank: e.tensor_copy(
            xfm[:, :, i * P:(i + 1) * P], ptr[:, b, :, :]).then_inc(e_ps.h, 1))
        ptr_rel[bank] = (e_ps, e_ps.inc())
        bank2 = (i + 1) % 2
        tv2 = transpose_pack(bank2, [xtmb[:, 1, c * P:(c + 1) * P] for c in range(DC)], DC)
        wait("scalar", t2v, tv2)
        wait("vector", t2v, tv2)
        for c in range(DC):
            em("vector", lambda e, i=i, b=bank2, c=c: e.tensor_scalar(
                bufA[:, c, i * P:(i + 1) * P], ptr[:, b, c, :],
                gcols[:, c:c + 1], None, ALU.mult).then_inc(e_ps.h, 1))
            e_ps.inc()
        ptr_rel[bank2] = (e_ps, e_ps.n)
    P1A_DONE = e_ps.n

    wq_seq = [0]

    def wq_dma(src_ap):
        j = wq_seq[0]
        wq_seq[0] += 1
        s = j % 5
        if j - 5 in wq_cons:
            wait("gpsimd", t2v, wq_cons[j - 5])
        em("gpsimd", lambda e, s=s, a=src_ap: e.dma_start(out=wring[:, s, :, :], in_=a)
           .then_inc(d_wq[s].h, 16))
        return s, d_wq[s].dma()

    wq_cons = {}

    def qkva_mm(ws, wwv, rhs_fn, nslots=4):
        slots = acquire(nslots)
        wait("tensor", d_wq[ws], wwv)
        for dc in range(DC):
            for tg in range(nslots):
                last = dc == DC - 1 and tg == nslots - 1
                em("tensor", lambda e, ws=ws, dc=dc, s=slots[tg], r=rhs_fn(dc, tg),
                   st=(dc == 0), sp=(dc == DC - 1), l=last:
                   (e.matmul(pmm[:, s, :], wring[:, ws, dc, :], r, start=st, stop=sp)
                    .then_inc(t2v.h, 1) if l else
                    e.matmul(pmm[:, s, :], wring[:, ws, dc, :], r, start=st, stop=sp)))
        return slots, t2v.inc()

    carry_rhs = lambda dc, tg: s_tm[:, 2 * dc + tg // 2, (tg % 2) * 512:(tg % 2) * 512 + 512]
    wait("tensor", e_ps, P1C_VEC)
    for c in range(DC):
        ws, wwv = wq_dma(wq_d[3 * c])
        slots, tv = qkva_mm(ws, wwv, carry_rhs)
        wq_cons[wq_seq[0] - 1] = tv
        wait("vector", t2v, tv)
        for tg in range(4):
            em("vector", lambda e, tg=tg, s=slots[tg]: e.tensor_copy(
                kvb[:, tg * 512:(tg + 1) * 512], pmm[:, s, :]).then_inc(e_ps.h, 1))
            slot_rel[slots[tg]] = (e_ps, e_ps.inc())
        ws, wwv = wq_dma(wq_d[3 * c + 1])
        slots, tv = qkva_mm(ws, wwv, carry_rhs)
        wq_cons[wq_seq[0] - 1] = tv
        wait("scalar", t2v, tv)
        for tg in range(4):
            em("scalar", lambda e, tg=tg, s=slots[tg]: e.activation(
                ab[:, tg * 512:(tg + 1) * 512], pmm[:, s, :], AF.Sigmoid).then_inc(e_ps.h, 1))
            slot_rel[slots[tg]] = (e_ps, e_ps.inc())
        AEV = e_ps.n
        wait("vector", e_ps, AEV)
        em("vector", lambda e: e.tensor_tensor_scan(
            out=yb[:], data0=ab[:], data1=kvb[:], initial=0.0,
            op0=ALU.mult, op1=ALU.add).then_inc(s2v.h, 1))
        YC = s2v.inc()
        wait("scalar", s2v, YC)
        em("scalar", lambda e, c=c: e.activation(
            ics[c][:], yb[:, 2047:2048], AF.Identity, scale=flag[:, 0:1]).then_inc(s2v.h, 1))
        s2v.inc()
    IC_ALL = s2v.n
    if cap == "p2a":
        return _finish(nc, ctx, plan, em, wait, d_out, stg, e_ps, _dbgpack())

    own_rhs = lambda dc, tg: bufA[:, dc, tg * 512:tg * 512 + 512]
    wait("tensor", e_ps, P1A_DONE)

    def emit_s_transposes(c):
        wait("tensor", stg, stg.n)
        for half in range(2):
            bank = half
            tvs = transpose_pack(bank, [qb[:, (half * 8 + j) * P:(half * 8 + j + 1) * P]
                                        for j in range(8)], 8)
            wait("vector", t2v, tvs)
            em("vector", lambda e, c=c, h=half, b=bank: e.tensor_copy(
                s_tm[:, h * 8:(h + 1) * 8, c * P:(c + 1) * P], ptr[:, b, :, :])
               .then_inc(e_ps.h, 1))
            ptr_rel[bank] = (e_ps, e_ps.inc())

    for c in range(DC):
        ws, wwv = wq_dma(wq_d[3 * c])
        slots, tv = qkva_mm(ws, wwv, own_rhs)
        wq_cons[wq_seq[0] - 1] = tv
        wait("vector", t2v, tv)
        for tg in range(4):
            em("vector", lambda e, tg=tg, s=slots[tg]: e.tensor_copy(
                kvb[:, tg * 512:(tg + 1) * 512], pmm[:, s, :]).then_inc(e_ps.h, 1))
            slot_rel[slots[tg]] = (e_ps, e_ps.inc())
        ws, wwv = wq_dma(wq_d[3 * c + 1])
        slots, tv = qkva_mm(ws, wwv, own_rhs)
        wq_cons[wq_seq[0] - 1] = tv
        wait("scalar", t2v, tv)
        for tg in range(4):
            em("scalar", lambda e, tg=tg, s=slots[tg]: e.activation(
                ab[:, tg * 512:(tg + 1) * 512], pmm[:, s, :], AF.Sigmoid).then_inc(e_ps.h, 1))
            slot_rel[slots[tg]] = (e_ps, e_ps.inc())
        AEV = e_ps.n
        if c > 0:
            emit_s_transposes(c - 1)
        ws, wwv = wq_dma(wq_d[3 * c + 2])
        slots, tv = qkva_mm(ws, wwv, own_rhs)
        wq_cons[wq_seq[0] - 1] = tv
        wait("vector", t2v, tv)
        for tg in range(4):
            em("vector", lambda e, tg=tg, s=slots[tg]: e.tensor_copy(
                qb[:, tg * 512:(tg + 1) * 512], pmm[:, s, :]).then_inc(e_ps.h, 1))
            slot_rel[slots[tg]] = (e_ps, e_ps.inc())
        wait("vector", e_ps, AEV)
        wait("vector", s2v, IC_ALL)
        em("vector", lambda e, c=c: e.tensor_tensor_scan(
            out=yb[:], data0=ab[:], data1=kvb[:], initial=ics[c][:, 0:1],
            op0=ALU.mult, op1=ALU.add))
        em("vector", lambda e: e.tensor_tensor(qb[:], qb[:], yb[:], ALU.mult)
           .then_inc(stg.h, 1))
        stg.inc()
    rslots = acquire(1)
    RS = rslots[0]
    for j in range(NT):
        for dc in range(DC):
            last = j == NT - 1 and dc == DC - 1
            em("tensor", lambda e, j=j, dc=dc, s=RS, st=(dc == 0), sp=(dc == DC - 1), l=last:
               (e.matmul(pmm[:, s, j * NE:(j + 1) * NE], xfm[:, dc, j * P:(j + 1) * P],
                         wrsb[:, dc, :], start=st, stop=sp).then_inc(t2v.h, 1) if l else
                e.matmul(pmm[:, s, j * NE:(j + 1) * NE], xfm[:, dc, j * P:(j + 1) * P],
                         wrsb[:, dc, :], start=st, stop=sp)))
    RTV = t2v.inc()
    emit_s_transposes(DC - 1)
    lgP = lambda j: pmm[:, RS, j * NE:(j + 1) * NE]

    def ack_vs(v_count):
        wait("scalar", v2s, v_count)
        em("scalar", lambda e: e.nop().then_inc(s2v.h, 1))
        a = s2v.inc()
        wait("vector", s2v, a)

    wait("vector", t2v, RTV)
    for j in range(NT):
        em("vector", lambda e, j=j: e.reduce_max(colbuf[:, j, 0:1], lgP(j), AX.X))
    for j in range(NT):
        last = j == NT - 1
        em("vector", lambda e, j=j, l=last:
           (e.tensor_scalar(colbuf[:, j, 1:2], colbuf[:, j, 0:1], -1.0, None, ALU.mult)
            .then_inc(v2s.h, 1) if l else
            e.tensor_scalar(colbuf[:, j, 1:2], colbuf[:, j, 0:1], -1.0, None, ALU.mult)))
    ack_vs(v2s.inc())
    for j in range(NT):
        em("vector", lambda e, j=j: e.tensor_scalar(
            w8[:, j, :], lgP(j), colbuf[:, j, 0:1], None, ALU.is_ge))
    for j in range(NT):
        em("vector", lambda e, j=j: e.scalar_tensor_tensor(
            out=w8[:, j, :], in0=w8[:, j, :], scalar=-1e30, in1=lgP(j),
            op0=ALU.mult, op1=ALU.add))
    for j in range(NT):
        last = j == NT - 1
        em("vector", lambda e, j=j, l=last:
           (e.reduce_max(colbuf[:, j, 2:3], w8[:, j, :], AX.X).then_inc(v2s.h, 1)
            if l else e.reduce_max(colbuf[:, j, 2:3], w8[:, j, :], AX.X)))
    W8V = v2s.inc()
    wait("scalar", v2s, W8V)
    for j in range(NT):
        em("scalar", lambda e, j=j: e.activation(
            colbuf[:, j, 3:4], colbuf[:, j, 2:3], AF.Exp, bias=colbuf[:, j, 1:2]))
    for j in range(NT):
        last = j == NT - 1
        em("scalar", lambda e, j=j, l=last:
           (e.activation(gtmp[:, j * NE:(j + 1) * NE], lgP(j), AF.Exp,
                         bias=colbuf[:, j, 1:2]).then_inc(s2v.h, 1) if l else
            e.activation(gtmp[:, j * NE:(j + 1) * NE], lgP(j), AF.Exp,
                         bias=colbuf[:, j, 1:2])))
    W8S = s2v.inc()
    wait("vector", s2v, W8S)
    for j in range(NT):
        em("vector", lambda e, j=j: e.tensor_scalar(
            colbuf[:, j, 4:5], colbuf[:, j, 3:4], 1.0, None, ALU.add))
    for j in range(NT):
        last = j == NT - 1
        em("vector", lambda e, j=j, l=last:
           (e.reciprocal(colbuf[:, j, 5:6], colbuf[:, j, 4:5]).then_inc(v2s.h, 1)
            if l else e.reciprocal(colbuf[:, j, 5:6], colbuf[:, j, 4:5])))
    ack_vs(v2s.inc())
    for j in range(NT):
        em("vector", lambda e, j=j: e.tensor_scalar(
            w8[:, j, :], lgP(j), colbuf[:, j, 2:3], None, ALU.is_ge))
    for j in range(NT):
        last = j == NT - 1
        em("vector", lambda e, j=j, l=last:
           (e.scalar_tensor_tensor(
               out=w8[:, j, :], in0=gtmp[:, j * NE:(j + 1) * NE], scalar=colbuf[:, j, 5:6],
               in1=w8[:, j, :], op0=ALU.mult, op1=ALU.mult).then_inc(e_ps.h, 1) if l else
            e.scalar_tensor_tensor(
               out=w8[:, j, :], in0=gtmp[:, j * NE:(j + 1) * NE], scalar=colbuf[:, j, 5:6],
               in1=w8[:, j, :], op0=ALU.mult, op1=ALU.mult)))
    W8DONE_EPS = e_ps.inc()
    slot_rel[RS] = (e_ps, W8DONE_EPS)
    wait("scalar", e_ps, W8DONE_EPS)
    em("scalar", lambda e: e.nop().then_inc(s2v.h, 1))
    W8ACK = s2v.inc()
    wait("vector", s2v, W8ACK)

    if cap == "p25":
        return _finish(nc, ctx, plan, em, wait, d_out, stg, e_ps, _dbgpack())
    wg_slots = [0, 2, 4]
    wu_slots = [1, 3]
    W_LAST_T2V = t2v.n
    wgwu_seq = [0]
    gu_cons = {}
    ring_tenant = {0: None, 1: None, 2: None, 3: None, 4: None}

    def stream_chunk(src_ap, slot):
        k = wgwu_seq[0]
        wgwu_seq[0] += 1
        t = ring_tenant[slot]
        if t is not None:
            wait("gpsimd", t2v, gu_cons[t])
        else:
            wait("gpsimd", t2v, W_LAST_T2V)
        ring_tenant[slot] = k
        em("gpsimd", lambda e, s=slot, a=src_ap: e.dma_start(out=wring[:, s, :, :], in_=a)
           .then_inc(d_wq[s].h, 16))
        return d_wq[slot].dma()

    down_done_t2v = [0]

    def wd_stream_q(phase, e_, q):
        if q == 0 and phase > 0:
            wait("gpsimd", t2v, down_done_t2v[0])
        em("gpsimd", lambda e, e_=e_, q=q: e.dma_start(
            out=wd_sb[:, 4 * q:4 * q + 4, :], in_=wd_d[e_][:, 4 * q:4 * q + 4, :])
           .then_inc(d_wd.h, 16))
        d_wd.dma()
        return d_wd.n

    def wd_stream(phase, e_):
        for q in range(4):
            wd_stream_q(phase, e_, q)
        return d_wd.n

    wd_wait = [wd_stream(0, 0)]

    silu_rel = [None, None]
    acc_guard = [0] * 8
    deferred_fin = []

    n_phases = 16 if cap in ("full", "p3_noln") else int(cap.split("_")[1])
    for phase in range(n_phases):
        half = phase // 8
        e_ = phase % 8
        tok0 = half * 1024
        for fb in range(FC):
            if 2 <= fb <= 5 and phase > 0:
                wd_wait[0] = wd_stream_q(phase, e_, fb - 2)
            wgs = wg_slots[(phase * FC + fb) % 3]
            wus = wu_slots[(phase * FC + fb) % 2]
            wgv = stream_chunk(wg_d[e_, fb], wgs)
            wuv = stream_chunk(wu_d[e_, fb], wus)
            gslots = acquire(2)
            wait("tensor", d_wq[wgs], wgv)
            for dc in range(DC):
                for tg in range(2):
                    last = dc == DC - 1 and tg == 1
                    em("tensor", lambda e, ws=wgs, dc=dc, s=gslots[tg],
                       r=xfm[:, dc, tok0 + tg * 512: tok0 + tg * 512 + 512],
                       st=(dc == 0), sp=(dc == DC - 1), l=last:
                       (e.matmul(pmm[:, s, :], wring[:, ws, dc, :], r, start=st, stop=sp)
                        .then_inc(t2v.h, 1) if l else
                        e.matmul(pmm[:, s, :], wring[:, ws, dc, :], r, start=st, stop=sp)))
            GTV = t2v.inc()
            gu_cons[wgwu_seq[0] - 2] = GTV
            uslots = acquire(2)
            wait("tensor", d_wq[wus], wuv)
            for dc in range(DC):
                for tg in range(2):
                    last = dc == DC - 1 and tg == 1
                    em("tensor", lambda e, ws=wus, dc=dc, s=uslots[tg],
                       r=xfm[:, dc, tok0 + tg * 512: tok0 + tg * 512 + 512],
                       st=(dc == 0), sp=(dc == DC - 1), l=last:
                       (e.matmul(pmm[:, s, :], wring[:, ws, dc, :], r, start=st, stop=sp)
                        .then_inc(t2v.h, 1) if l else
                        e.matmul(pmm[:, s, :], wring[:, ws, dc, :], r, start=st, stop=sp)))
            UTV = t2v.inc()
            gu_cons[wgwu_seq[0] - 1] = UTV
            wait("scalar", t2v, GTV)
            if silu_rel[0] is not None:
                wait("scalar", silu_rel[0][0], silu_rel[0][1])
            em("scalar", lambda e, s=gslots[0]: e.activation(
                xtmb[:, 0, 0:512], pmm[:, s, :], AF.Silu).then_inc(e_ps.h, 1))
            slot_rel[gslots[0]] = (e_ps, e_ps.inc())
            if silu_rel[1] is not None:
                wait("scalar", silu_rel[1][0], silu_rel[1][1])
            em("scalar", lambda e, s=gslots[1]: e.activation(
                xtmb[:, 0, 512:1024], pmm[:, s, :], AF.Silu).then_inc(s2v.h, 1))
            SILU = s2v.inc()
            slot_rel[gslots[1]] = (s2v, SILU)
            wait("vector", s2v, SILU)
            wait("vector", t2v, UTV)
            for tg in range(2):
                hsl = bufA[:, fb // 2, (fb % 2) * 1024 + tg * 512:(fb % 2) * 1024 + tg * 512 + 512]
                em("vector", lambda e, s=uslots[tg], tg=tg, h=hsl: e.tensor_tensor(
                    h, xtmb[:, 0, tg * 512:(tg + 1) * 512], pmm[:, s, :], ALU.mult)
                   .then_inc(e_ps.h, 1))
                slot_rel[uslots[tg]] = (e_ps, e_ps.inc())
                silu_rel[tg] = (e_ps, e_ps.n)
        H_DONE = e_ps.n
        wait("tensor", d_wd, wd_wait[0])
        wait("tensor", e_ps, H_DONE)
        def finalize_pair(fhalf, tcs, acc_done):
            cbs = [fhalf * 8 + tc for tc in tcs]
            gws = [gtmp[:] if tc % 2 == 0 else ab[:, 1024:2048] for tc in tcs]
            wait("scalar", e_ps, acc_done)
            for tc, cb in zip(tcs, cbs):
                last = tc == tcs[-1]
                em("scalar", lambda e, tc=tc, cb=cb, l=last:
                   (e.activation(ab[:, 0:1024], acc[:, tc, :], AF.Identity,
                                 accum_out=colbuf[:, cb, SUM:SUM + 1]).then_inc(s2v.h, 1)
                    if l else
                    e.activation(ab[:, 0:1024], acc[:, tc, :], AF.Identity,
                                 accum_out=colbuf[:, cb, SUM:SUM + 1])))
            wait("vector", s2v, s2v.inc())
            for cb in cbs:
                last = cb == cbs[-1]
                em("vector", lambda e, cb=cb, l=last:
                   (e.tensor_scalar(colbuf[:, cb, NEGM:NEGM + 1], colbuf[:, cb, SUM:SUM + 1],
                                    -1.0 / D, None, ALU.mult).then_inc(v2s.h, 1) if l else
                    e.tensor_scalar(colbuf[:, cb, NEGM:NEGM + 1], colbuf[:, cb, SUM:SUM + 1],
                                    -1.0 / D, None, ALU.mult)))
            wait("scalar", v2s, v2s.inc())
            for tc, cb in zip(tcs, cbs):
                last = tc == tcs[-1]
                em("scalar", lambda e, tc=tc, cb=cb, l=last:
                   (e.activation(ab[:, 0:1024], acc[:, tc, :], AF.Square,
                                 bias=colbuf[:, cb, NEGM:NEGM + 1],
                                 accum_out=colbuf[:, cb, VS:VS + 1]).then_inc(s2v.h, 1)
                    if l else
                    e.activation(ab[:, 0:1024], acc[:, tc, :], AF.Square,
                                 bias=colbuf[:, cb, NEGM:NEGM + 1],
                                 accum_out=colbuf[:, cb, VS:VS + 1])))
            wait("vector", s2v, s2v.inc())
            for cb in cbs:
                last = cb == cbs[-1]
                em("vector", lambda e, cb=cb, l=last:
                   (e.tensor_scalar(colbuf[:, cb, V:V + 1], colbuf[:, cb, VS:VS + 1],
                                    1.0 / D, None, ALU.mult).then_inc(v2s.h, 1) if l else
                    e.tensor_scalar(colbuf[:, cb, V:V + 1], colbuf[:, cb, VS:VS + 1],
                                    1.0 / D, None, ALU.mult)))
            wait("scalar", v2s, v2s.inc())
            for cb in cbs:
                last = cb == cbs[-1]
                em("scalar", lambda e, cb=cb, l=last:
                   (e.activation(colbuf[:, cb, STD1:STD1 + 1], colbuf[:, cb, V:V + 1],
                                 AF.Sqrt, bias=epsc[:, 0:1]).then_inc(s2v.h, 1) if l else
                    e.activation(colbuf[:, cb, STD1:STD1 + 1], colbuf[:, cb, V:V + 1],
                                 AF.Sqrt, bias=epsc[:, 0:1])))
            wait("vector", s2v, s2v.inc())
            for cb in cbs:
                em("vector", lambda e, cb=cb: e.reciprocal(
                    colbuf[:, cb, RS1:RS1 + 1], colbuf[:, cb, STD1:STD1 + 1]))
            for cb in cbs:
                em("vector", lambda e, cb=cb: e.tensor_scalar(
                    colbuf[:, cb, SPARE:SPARE + 1], colbuf[:, cb, SUM:SUM + 1],
                    0.0, None, ALU.mult))
            for cb in cbs:
                last = cb == cbs[-1]
                em("vector", lambda e, cb=cb, l=last:
                   (e.tensor_tensor(colbuf[:, cb, NM1:NM1 + 1], colbuf[:, cb, NEGM:NEGM + 1],
                                    colbuf[:, cb, RS1:RS1 + 1], ALU.mult).then_inc(v2s.h, 1)
                    if l else
                    e.tensor_tensor(colbuf[:, cb, NM1:NM1 + 1], colbuf[:, cb, NEGM:NEGM + 1],
                                    colbuf[:, cb, RS1:RS1 + 1], ALU.mult)))
            wait("scalar", v2s, v2s.inc())
            for tc, cb in zip(tcs, cbs):
                last = tc == tcs[-1]
                em("scalar", lambda e, tc=tc, cb=cb, l=last:
                   (e.activation(acc[:, tc, :], acc[:, tc, :], AF.Identity,
                                 bias=colbuf[:, cb, NM1:NM1 + 1],
                                 scale=colbuf[:, cb, RS1:RS1 + 1]).then_inc(s2v.h, 1)
                    if l else
                    e.activation(acc[:, tc, :], acc[:, tc, :], AF.Identity,
                                 bias=colbuf[:, cb, NM1:NM1 + 1],
                                 scale=colbuf[:, cb, RS1:RS1 + 1])))
            wait("vector", s2v, s2v.inc())
            for tc, g in zip(tcs, gws):
                em("vector", lambda e, tc=tc, g=g: e.tensor_tensor(
                    g, acc[:, tc, :], lng[:], ALU.mult))
            for g in gws:
                last = g is gws[-1]
                em("vector", lambda e, g=g, l=last:
                   (e.tensor_tensor(g, g, lnb[:], ALU.add).then_inc(v2s.h, 1) if l else
                    e.tensor_tensor(g, g, lnb[:], ALU.add)))
            wait("scalar", v2s, v2s.inc())
            for g in gws:
                last = g is gws[-1]
                em("scalar", lambda e, g=g, l=last:
                   (e.activation(g, g, AF.Silu).then_inc(s2v.h, 1) if l else
                    e.activation(g, g, AF.Silu)))
            wait("vector", s2v, s2v.inc())
            for tc, cb, g in zip(tcs, cbs, gws):
                em("vector", lambda e, tc=tc, cb=cb, g=g: e.tensor_tensor(
                    acc[:, tc, :], s_tm[:, cb, :], g, ALU.mult).then_inc(v2g.h, 1))
                v2g.inc()
            wait("gpsimd", v2g, v2g.n)
            for tc, cb in zip(tcs, cbs):
                em("gpsimd", lambda e, tile=cb, tc=tc: e.dma_start(
                    out=acc[:, tc, :], in_=x_own[tile], accum_op=ALU.add)
                   .then_inc(d_xa.h, 16))
                d_xa.dma()
            wait("sync", d_xa, d_xa.n)
            for tc, cb in zip(tcs, cbs):
                em("sync", lambda e, tile=cb, tc=tc: e.dma_start(
                    out=out_d[tile], in_=acc[:, tc, :]).then_inc(d_out.h, 16))
                acc_guard[tc] = d_out.dma()

        for tc in range(8):
            dslots = acquire(2)
            for fc in range(FC):
                lhs = bufA[:, fc // 2, (fc % 2) * 1024 + tc * P:(fc % 2) * 1024 + (tc + 1) * P]
                for dh in range(2):
                    last = fc == FC - 1 and dh == 1
                    em("tensor", lambda e, l=lhs, fc=fc, s=dslots[dh], dh=dh,
                       st=(fc == 0), sp=(fc == FC - 1), l2=last:
                       (e.matmul(pmm[:, s, :], l, wd_sb[:, fc, dh * 512:(dh + 1) * 512],
                                 start=st, stop=sp).then_inc(t2v.h, 1) if l2 else
                        e.matmul(pmm[:, s, :], l, wd_sb[:, fc, dh * 512:(dh + 1) * 512],
                                 start=st, stop=sp)))
            DTV = t2v.inc()
            wait("vector", t2v, DTV)
            if e_ == 0:
                wait("vector", d_out, acc_guard[tc])
            for dh in range(2):
                aslice = acc[:, tc, dh * 512:(dh + 1) * 512]
                wcol = w8[:, half * 8 + tc, e_:e_ + 1]
                if e_ == 0:
                    em("vector", lambda e, s=dslots[dh], a=aslice, w=wcol: e.tensor_scalar(
                        a, pmm[:, s, :], w, None, ALU.mult).then_inc(e_ps.h, 1))
                else:
                    em("vector", lambda e, s=dslots[dh], a=aslice, w=wcol: e.scalar_tensor_tensor(
                        out=a, in0=pmm[:, s, :], scalar=w, in1=a,
                        op0=ALU.mult, op1=ALU.add).then_inc(e_ps.h, 1))
                slot_rel[dslots[dh]] = (e_ps, e_ps.inc())
            if e_ == 7 and tc % 2 == 1:
                finalize_pair(half, (tc - 1, tc), e_ps.n)
        down_done_t2v[0] = t2v.n

    return _finish(nc, ctx, plan, em, wait, d_out, stg, e_ps, _dbgpack())


def _finish(nc, ctx, plan, em, wait, d_out, stg, e_ps, dbg_pack=None):
    if dbg_pack is not None:
        dbg_o, srcs, d_dbg = dbg_pack
        for nm, src in srcs.items():
            em("sync", lambda e, o=dbg_o[nm], s=src: e.dma_start(out=o, in_=s)
               .then_inc(d_dbg.h, 16))
            d_dbg.dma()
        wait("sync", d_dbg, d_dbg.n)
    if d_out.n > 0:
        wait("sync", d_out, d_out.n)
    else:
        wait("sync", e_ps, e_ps.n)
        wait("sync", stg, stg.n)

    block = ctx.enter_context(nc.Block())

    @block.sync
    def _(e):
        for fn in plan["sync"]:
            fn(e)

    @block.gpsimd
    def _(e):
        for fn in plan["gpsimd"]:
            fn(e)

    @block.tensor
    def _(e):
        for fn in plan["tensor"]:
            fn(e)

    @block.vector
    def _(e):
        for fn in plan["vector"]:
            fn(e)

    @block.scalar
    def _(e):
        for fn in plan["scalar"]:
            fn(e)

    ctx.close()
    return nc


def _prep_weights(W_qkva, w_gate, w_up, w_down, w_router, gamma, ln_gamma, ln_beta):
    f32 = np.float32
    WQ = np.asarray(W_qkva, f32)
    A = WQ.reshape(24, P, DC, P).transpose(0, 3, 2, 1)
    order = []
    for c in range(DC):
        order += [8 + c, 16 + c, c]
    wq_l = np.ascontiguousarray(A[order])
    G = np.asarray(w_gate, f32).reshape(NE, FC, P, DC, P)
    wg_l = np.ascontiguousarray(G.transpose(0, 1, 4, 3, 2))
    U = np.asarray(w_up, f32).reshape(NE, FC, P, DC, P)
    wu_l = np.ascontiguousarray(U.transpose(0, 1, 4, 3, 2))
    wd_l = np.ascontiguousarray(np.asarray(w_down, f32).transpose(0, 2, 1)
                                .reshape(NE, FC, P, D).transpose(0, 2, 1, 3))
    wr_l = np.ascontiguousarray(np.asarray(w_router, f32).T.reshape(DC, P, NE)
                                .transpose(1, 0, 2))
    gc_l = np.ascontiguousarray(np.asarray(gamma, f32).reshape(DC, P).T)
    lng_l = np.ascontiguousarray(np.broadcast_to(np.asarray(ln_gamma, f32), (P, D)))
    lnb_l = np.ascontiguousarray(np.broadcast_to(np.asarray(ln_beta, f32), (P, D)))
    id_l = np.eye(P, dtype=ml_dtypes.bfloat16)
    return wq_l, wg_l, wu_l, wd_l, wr_l, gc_l, lng_l, lnb_l, id_l


def kernel(L0_attn_x, L0_attn_W_qkva, L0_attn_gamma, L0_ffn_w_router,
           L0_ffn_w_gate, L0_ffn_w_up, L0_ffn_w_down, L0_ffn_ln_gamma,
           L0_ffn_ln_beta):
    x = np.ascontiguousarray(np.asarray(L0_attn_x, np.float32))
    wq_l, wg_l, wu_l, wd_l, wr_l, gc_l, lng_l, lnb_l, id_l = _prep_weights(
        L0_attn_W_qkva, L0_ffn_w_gate, L0_ffn_w_up, L0_ffn_w_down,
        L0_ffn_w_router, L0_attn_gamma, L0_ffn_ln_gamma, L0_ffn_ln_beta)

    nc = build_program()
    in_maps = []
    for c in range(8):
        b, h = c // 2, c % 2
        xo = np.ascontiguousarray(x[b, h * 2048:(h + 1) * 2048].reshape(NT, P, D))
        xc = np.ascontiguousarray(x[b, 0:2048].reshape(NT, P, D)) if h == 1 else xo
        fl = np.full((P, 1), 1.0 if h == 1 else 0.0, np.float32)
        in_maps.append({
            "x_own": xo, "x_carry": xc, "flag_d": fl,
            "wq_d": wq_l, "wg_d": wg_l, "wu_d": wu_l, "wd_d": wd_l,
            "wr_d": wr_l, "gc_d": gc_l, "lng_d": lng_l, "lnb_d": lnb_l,
            "id_d": id_l,
        })
    res = run_bass_kernel_spmd(nc, in_maps, core_ids=list(range(8)))
    out = np.empty((4, 4096, 1024), np.float32)
    for c in range(8):
        b, h = c // 2, c % 2
        out[b, h * 2048:(h + 1) * 2048] = np.asarray(
            res.results[c]["out_d"], np.float32).reshape(2048, 1024)
    return out


if __name__ == "__main__":
    import reference
    inp = reference.setup_inputs(0)
    got = kernel(**{k: np.asarray(v) for k, v in inp.items()})
    ref = np.asarray(reference.reference(**inp))
    rvr = float(((got - ref) ** 2).mean() / (ref ** 2).mean())
    print("residual variance ratio:", rvr)


# version 20
# speedup vs baseline: 1.0214x; 1.0214x over previous
import os
import numpy as np
import ml_dtypes
import concourse.bass as bass
import concourse.mybir as mybir
from concourse.bass_utils import run_bass_kernel_spmd
from contextlib import ExitStack

F32 = mybir.dt.float32
BF16 = mybir.dt.bfloat16
AF = mybir.ActivationFunctionType
ALU = mybir.AluOpType
AX = mybir.AxisListType

P = 128
NT = 16
D = 1024
DC = 8
FF = 2048
FC = 16
NE = 8
LN_EPS = 1e-5

SUM, VS, NEGM, V, STD1, STD0, RS1, RS0, NM1, NM0, R2, SPARE = range(12)


class Sem:
    def __init__(self, h):
        self.h = h
        self.n = 0

    def inc(self, k=1):
        self.n += k
        return self.n

    def dma(self):
        self.n += 16
        return self.n


def build_program(cap="full"):
    nc = bass.Bass("TRN2", target_bir_lowering=False, detect_race_conditions=False)

    dri = lambda n, s, d=F32: nc.dram_tensor(n, s, d, kind="ExternalInput").ap()
    x_own = dri("x_own", [NT, P, D])
    x_carry = dri("x_carry", [NT, P, D])
    flag_d = dri("flag_d", [P, 1])
    wq_d = dri("wq_d", [24, P, DC, P])
    wg_d = dri("wg_d", [NE, FC, P, DC, P])
    wu_d = dri("wu_d", [NE, FC, P, DC, P])
    wd_d = dri("wd_d", [NE, P, FC, D])
    wr_d = dri("wr_d", [P, DC, NE])
    gc_d = dri("gc_d", [P, DC])
    lng_d = dri("lng_d", [P, D])
    lnb_d = dri("lnb_d", [P, D])
    id_d = dri("id_d", [P, P], BF16)
    out_d = nc.dram_tensor("out_d", [NT, P, D], F32, kind="ExternalOutput").ap()
    dbg = os.environ.get("KDBG", "0") == "1"
    if dbg:
        dbg_o = {
            "xfm_o": nc.dram_tensor("xfm_o", [P, DC, 2048], BF16, kind="ExternalOutput").ap(),
            "bufA_o": nc.dram_tensor("bufA_o", [P, DC, 2048], BF16, kind="ExternalOutput").ap(),
            "stm_o": nc.dram_tensor("stm_o", [P, NT, D], BF16, kind="ExternalOutput").ap(),
            "acc_o": nc.dram_tensor("acc_o", [P, 8, D], F32, kind="ExternalOutput").ap(),
            "w8_o": nc.dram_tensor("w8_o", [P, NT, NE], F32, kind="ExternalOutput").ap(),
            "cols_o": nc.dram_tensor("cols_o", [P, NT, 12], F32, kind="ExternalOutput").ap(),
            "ab_o": nc.dram_tensor("ab_o", [P, 2048], F32, kind="ExternalOutput").ap(),
            "yb_o": nc.dram_tensor("yb_o", [P, 2048], BF16, kind="ExternalOutput").ap(),
            "kvb_o": nc.dram_tensor("kvb_o", [P, 2048], BF16, kind="ExternalOutput").ap(),
            "qb_o": nc.dram_tensor("qb_o", [P, 2048], BF16, kind="ExternalOutput").ap(),
        }
    else:
        dbg_o = None

    ctx = ExitStack()
    sb = lambda n, s, d: ctx.enter_context(nc.sbuf_tensor(n, s, d))

    xfm = sb("xfm", [P, DC, 2048], BF16)
    bufA = sb("bufA", [P, DC, 2048], BF16)
    s_tm = sb("s_tm", [P, NT, D], BF16)
    acc = sb("acc", [P, 8, D], F32)
    wd_sb = sb("wd_sb", [P, FC, D], BF16)
    wring = sb("wring", [P, 5, DC, P], BF16)
    kvb = sb("kvb", [P, 2048], BF16)
    qb = sb("qb", [P, 2048], BF16)
    ab = sb("ab", [P, 2048], F32)
    yb = sb("yb", [P, 2048], BF16)
    gtmp = sb("gtmp", [P, D], F32)
    xtmb = sb("xtmb", [P, 2, D], BF16)
    lng = sb("lng", [P, D], BF16)
    lnb = sb("lnb", [P, D], BF16)
    wrsb = sb("wrsb", [P, DC, NE], BF16)
    w8 = sb("w8", [P, NT, NE], F32)
    ident = sb("ident", [P, P], BF16)
    gcols = sb("gcols", [P, DC], F32)
    flag = sb("flag", [P, 1], F32)
    colbuf = sb("colbuf", [P, NT, 12], F32)
    epsc = sb("epsc", [P, 1], F32)
    ics = [sb(f"ic{c}", [P, 1], F32) for c in range(DC)]

    pmm = ctx.enter_context(nc.psum_tensor("pmm", [P, 6, 512], F32))
    ptr = ctx.enter_context(nc.psum_tensor("ptr", [P, 2, DC, P], BF16))

    sem = lambda n: Sem(ctx.enter_context(nc.semaphore(n)))
    d_cst = sem("d_cst")
    d_cstg = sem("d_cstg")
    d_x = [sem("d_x0"), sem("d_x1")]
    f_x = sem("f_x")
    d_wq = [sem(f"d_wq{i}") for i in range(5)]
    f_wq = sem("f_wq")
    d_wd = sem("d_wd")
    f_wd = sem("f_wd")
    d_xa = sem("d_xa")
    d_out = sem("d_out")
    t2v = sem("t2v")
    e_ps = sem("e_ps")
    s2v = sem("s2v")
    v2s = sem("v2s")
    stg = sem("stg")
    v2g = sem("v2g")

    d_dbg = sem("d_dbg")

    def _dbgpack():
        if not dbg:
            return None
        srcs = {"xfm_o": xfm[:], "bufA_o": bufA[:], "stm_o": s_tm[:],
                "acc_o": acc[:], "w8_o": w8[:], "cols_o": colbuf[:],
                "ab_o": ab[:], "yb_o": yb[:], "kvb_o": kvb[:], "qb_o": qb[:]}
        return (dbg_o, srcs, d_dbg)

    plan = {k: [] for k in ("sync", "gpsimd", "tensor", "vector", "scalar")}

    def em(eng, fn):
        plan[eng].append(fn)

    def wait(eng, s, v):
        if v > 0:
            em(eng, lambda e, s=s, v=v: e.wait_ge(s.h, v))

    slot_rel = [None] * 6
    ptr_rel = [None] * 2
    slot_ctr = [0]

    def acquire(n):
        out = []
        for _ in range(n):
            s = slot_ctr[0] % 6
            slot_ctr[0] += 1
            if slot_rel[s] is not None:
                wait("tensor", slot_rel[s][0], slot_rel[s][1])
            out.append(s)
        return out

    em("sync", lambda e: e.dma_start(out=ident[:], in_=id_d).then_inc(d_cst.h, 16)); d_cst.dma()
    em("sync", lambda e: e.dma_start(out=gcols[:], in_=gc_d).then_inc(d_cst.h, 16)); d_cst.dma()
    em("sync", lambda e: e.dma_start(out=flag[:], in_=flag_d).then_inc(d_cst.h, 16)); d_cst.dma()
    CST_ALL = d_cst.n
    em("gpsimd", lambda e: e.dma_start(out=lng[:], in_=lng_d).then_inc(d_cstg.h, 16)); d_cstg.dma()
    em("gpsimd", lambda e: e.dma_start(out=lnb[:], in_=lnb_d).then_inc(d_cstg.h, 16)); d_cstg.dma()
    em("gpsimd", lambda e: e.dma_start(out=wrsb[:], in_=wr_d).then_inc(d_cstg.h, 16)); d_cstg.dma()
    CSTG_ALL = d_cstg.n
    wait("tensor", d_cst, CST_ALL)
    wait("vector", d_cst, CST_ALL)
    wait("scalar", d_cst, CST_ALL)
    em("vector", lambda e: e.memset(epsc[:], LN_EPS).then_inc(v2s.h, 1))
    v2s.inc()

    x_seq = [0]
    xfree = []

    def x_dma(src_ap):
        i = x_seq[0]
        x_seq[0] += 1
        slot = i % 2
        if i >= 2:
            for sm, vv in xfree[i - 2]:
                wait("sync", sm, vv)
        em("sync", lambda e, s=slot, a=src_ap: e.dma_start(out=acc[:, 1 + s, :], in_=a)
           .then_inc(d_x[s].h, 16))
        return slot, d_x[slot].dma()

    def tile_stats(i, slot, wv, xn_also):
        wait("scalar", d_x[slot], wv)
        em("scalar", lambda e, i=i, s=slot: e.activation(
            acc[:, 0, :], acc[:, 1 + s, :], AF.Identity,
            accum_out=colbuf[:, i, SUM:SUM + 1]).then_inc(s2v.h, 1))
        s2v.inc()
        wait("vector", s2v, s2v.n)
        em("vector", lambda e, i=i: e.tensor_scalar(
            colbuf[:, i, NEGM:NEGM + 1], colbuf[:, i, SUM:SUM + 1], -1.0 / D, None,
            ALU.mult).then_inc(v2s.h, 1))
        v2s.inc()
        wait("scalar", v2s, v2s.n)
        em("scalar", lambda e, i=i, s=slot: e.activation(
            acc[:, 0, :], acc[:, 1 + s, :], AF.Square, bias=colbuf[:, i, NEGM:NEGM + 1],
            accum_out=colbuf[:, i, VS:VS + 1]).then_inc(s2v.h, 1))
        s2v.inc()
        wait("vector", s2v, s2v.n)
        em("vector", lambda e, i=i: e.tensor_scalar(
            colbuf[:, i, V:V + 1], colbuf[:, i, VS:VS + 1], 1.0 / D, None,
            ALU.mult).then_inc(v2s.h, 1))
        v2s.inc()
        wait("scalar", v2s, v2s.n)
        em("scalar", lambda e, i=i: e.activation(
            colbuf[:, i, STD1:STD1 + 1], colbuf[:, i, V:V + 1], AF.Sqrt, bias=epsc[:, 0:1]))
        em("scalar", lambda e, i=i: e.activation(
            colbuf[:, i, STD0:STD0 + 1], colbuf[:, i, V:V + 1], AF.Sqrt).then_inc(s2v.h, 1))
        s2v.inc()
        wait("vector", s2v, s2v.n)
        em("vector", lambda e, i=i: e.reciprocal(colbuf[:, i, RS1:RS1 + 1], colbuf[:, i, STD1:STD1 + 1]))
        em("vector", lambda e, i=i: e.reciprocal(colbuf[:, i, RS0:RS0 + 1], colbuf[:, i, STD0:STD0 + 1]))
        em("vector", lambda e, i=i: e.tensor_scalar(
            colbuf[:, i, SPARE:SPARE + 1], colbuf[:, i, SUM:SUM + 1], 0.0, None, ALU.mult))
        if xn_also:
            em("vector", lambda e, i=i: e.tensor_tensor(
                colbuf[:, i, NM0:NM0 + 1], colbuf[:, i, NEGM:NEGM + 1], colbuf[:, i, RS0:RS0 + 1], ALU.mult))
            em("vector", lambda e, i=i: e.tensor_tensor(
                colbuf[:, i, NM1:NM1 + 1], colbuf[:, i, NEGM:NEGM + 1], colbuf[:, i, RS1:RS1 + 1], ALU.mult))
            em("vector", lambda e, i=i: e.tensor_tensor(
                colbuf[:, i, R2:R2 + 1], colbuf[:, i, RS0:RS0 + 1],
                colbuf[:, i, STD1:STD1 + 1], ALU.mult).then_inc(v2s.h, 1))
        else:
            em("vector", lambda e, i=i: e.tensor_tensor(
                colbuf[:, i, NM0:NM0 + 1], colbuf[:, i, NEGM:NEGM + 1],
                colbuf[:, i, RS0:RS0 + 1], ALU.mult).then_inc(v2s.h, 1))
        v2s.inc()
        wait("scalar", v2s, v2s.n)
        if xn_also:
            em("scalar", lambda e, i=i, s=slot: e.activation(
                xtmb[:, 0, :], acc[:, 1 + s, :], AF.Identity,
                bias=colbuf[:, i, NM1:NM1 + 1], scale=colbuf[:, i, RS1:RS1 + 1]).then_inc(s2v.h, 1))
            xfree.append((s2v, s2v.inc()))
            wait("vector", s2v, s2v.n)
            em("vector", lambda e, i=i: e.tensor_scalar(
                xtmb[:, 1, :], xtmb[:, 0, :], colbuf[:, i, R2:R2 + 1], None, ALU.mult)
               .then_inc(stg.h, 1))
            stg.inc()
        else:
            em("scalar", lambda e, i=i, s=slot: e.activation(
                xtmb[:, 1, :], acc[:, 1 + s, :], AF.Identity,
                bias=colbuf[:, i, NM0:NM0 + 1], scale=colbuf[:, i, RS0:RS0 + 1]).then_inc(stg.h, 1))
            xfree.append([(stg, stg.inc())])

    def transpose_pack(bank, src_slices, n):
        if ptr_rel[bank] is not None:
            wait("tensor", ptr_rel[bank][0], ptr_rel[bank][1])
        for j in range(n):
            last = j == n - 1
            em("tensor", lambda e, b=bank, j=j, sl=src_slices[j], l=last:
               (e.transpose(ptr[:, b, j, :], sl, ident[:]).then_inc(t2v.h, 1)
                if l else e.transpose(ptr[:, b, j, :], sl, ident[:])))
        return t2v.inc()

    if cap == "p0":
        return _finish(nc, ctx, plan, em, wait, d_out, stg, e_ps, _dbgpack())
    SQ, M, MSQ = VS, NEGM, SPARE

    def stats_region(x_src, xn_also):
        for i in range(NT):
            slot, wv = x_dma(x_src[i])
            wait("vector", d_x[slot], wv)
            em("vector", lambda e, i=i, s=slot: e.reduce_sum(
                colbuf[:, i, SUM:SUM + 1], acc[:, 1 + s, :], AX.X).then_inc(v2s.h, 1))
            rsv = v2s.inc()
            wait("scalar", d_x[slot], wv)
            em("scalar", lambda e, i=i, s=slot: e.activation(
                acc[:, 0, :], acc[:, 1 + s, :], AF.Square,
                accum_out=colbuf[:, i, SQ:SQ + 1]).then_inc(s2v.h, 1))
            xfree.append([(s2v, s2v.inc()), (v2s, rsv)])
        wait("vector", s2v, s2v.n)
        for i in range(NT):
            em("vector", lambda e, i=i: e.tensor_scalar(
                colbuf[:, i, M:M + 1], colbuf[:, i, SUM:SUM + 1], 1.0 / D, None, ALU.mult))
        for i in range(NT):
            em("vector", lambda e, i=i: e.tensor_tensor(
                colbuf[:, i, MSQ:MSQ + 1], colbuf[:, i, M:M + 1], colbuf[:, i, M:M + 1],
                ALU.mult))
        for i in range(NT):
            last = i == NT - 1
            em("vector", lambda e, i=i, l=last:
               (e.scalar_tensor_tensor(
                   out=colbuf[:, i, V:V + 1], in0=colbuf[:, i, SQ:SQ + 1], scalar=1.0 / D,
                   in1=colbuf[:, i, MSQ:MSQ + 1], op0=ALU.mult,
                   op1=ALU.subtract).then_inc(v2s.h, 1) if l else
                e.scalar_tensor_tensor(
                   out=colbuf[:, i, V:V + 1], in0=colbuf[:, i, SQ:SQ + 1], scalar=1.0 / D,
                   in1=colbuf[:, i, MSQ:MSQ + 1], op0=ALU.mult, op1=ALU.subtract)))
        wait("scalar", v2s, v2s.inc())
        for i in range(NT):
            em("scalar", lambda e, i=i: e.activation(
                colbuf[:, i, STD1:STD1 + 1], colbuf[:, i, V:V + 1], AF.Sqrt,
                bias=epsc[:, 0:1]))
        for i in range(NT):
            last = i == NT - 1
            em("scalar", lambda e, i=i, l=last:
               (e.activation(colbuf[:, i, STD0:STD0 + 1], colbuf[:, i, V:V + 1],
                             AF.Sqrt).then_inc(s2v.h, 1) if l else
                e.activation(colbuf[:, i, STD0:STD0 + 1], colbuf[:, i, V:V + 1], AF.Sqrt)))
        wait("vector", s2v, s2v.inc())
        for i in range(NT):
            em("vector", lambda e, i=i: e.reciprocal(
                colbuf[:, i, RS1:RS1 + 1], colbuf[:, i, STD1:STD1 + 1]))
        for i in range(NT):
            em("vector", lambda e, i=i: e.reciprocal(
                colbuf[:, i, RS0:RS0 + 1], colbuf[:, i, STD0:STD0 + 1]))
        for i in range(NT):
            em("vector", lambda e, i=i: e.tensor_tensor(
                colbuf[:, i, NM0:NM0 + 1], colbuf[:, i, M:M + 1], colbuf[:, i, RS0:RS0 + 1],
                ALU.mult))
        if xn_also:
            for i in range(NT):
                em("vector", lambda e, i=i: e.tensor_tensor(
                    colbuf[:, i, NM1:NM1 + 1], colbuf[:, i, M:M + 1],
                    colbuf[:, i, RS1:RS1 + 1], ALU.mult))
            for i in range(NT):
                em("vector", lambda e, i=i: e.tensor_scalar(
                    colbuf[:, i, NM1:NM1 + 1], colbuf[:, i, NM1:NM1 + 1], -1.0, None,
                    ALU.mult))
            for i in range(NT):
                em("vector", lambda e, i=i: e.tensor_scalar(
                    colbuf[:, i, NM0:NM0 + 1], colbuf[:, i, NM0:NM0 + 1], -1.0, None,
                    ALU.mult))
            for i in range(NT):
                last = i == NT - 1
                em("vector", lambda e, i=i, l=last:
                   (e.tensor_tensor(colbuf[:, i, R2:R2 + 1], colbuf[:, i, RS0:RS0 + 1],
                                    colbuf[:, i, STD1:STD1 + 1], ALU.mult)
                    .then_inc(v2s.h, 1) if l else
                    e.tensor_tensor(colbuf[:, i, R2:R2 + 1], colbuf[:, i, RS0:RS0 + 1],
                                    colbuf[:, i, STD1:STD1 + 1], ALU.mult)))
        else:
            for i in range(NT):
                last = i == NT - 1
                em("vector", lambda e, i=i, l=last:
                   (e.tensor_scalar(colbuf[:, i, NM0:NM0 + 1], colbuf[:, i, NM0:NM0 + 1],
                                    -1.0, None, ALU.mult).then_inc(v2s.h, 1) if l else
                    e.tensor_scalar(colbuf[:, i, NM0:NM0 + 1], colbuf[:, i, NM0:NM0 + 1],
                                    -1.0, None, ALU.mult)))
        wait("scalar", v2s, v2s.inc())

    stats_region(x_carry, xn_also=False)
    for i in range(NT):
        slot, wv = x_dma(x_carry[i])
        wait("scalar", d_x[slot], wv)
        em("scalar", lambda e, i=i, s=slot: e.activation(
            xtmb[:, 1, :], acc[:, 1 + s, :], AF.Identity,
            bias=colbuf[:, i, NM0:NM0 + 1], scale=colbuf[:, i, RS0:RS0 + 1])
           .then_inc(stg.h, 1))
        xfree.append([(stg, stg.inc())])
        wait("tensor", stg, stg.n)
        bank = i % 2
        tv = transpose_pack(bank, [xtmb[:, 1, c * P:(c + 1) * P] for c in range(DC)], DC)
        wait("scalar", t2v, tv)
        wait("vector", t2v, tv)
        for c in range(DC):
            em("vector", lambda e, i=i, b=bank, c=c: e.tensor_scalar(
                s_tm[:, 2 * c + i // 8, (i % 8) * P:(i % 8 + 1) * P], ptr[:, b, c, :],
                gcols[:, c:c + 1], None, ALU.mult).then_inc(e_ps.h, 1))
            e_ps.inc()
        ptr_rel[bank] = (e_ps, e_ps.n)
    P1C_VEC = e_ps.n
    if cap == "p1c":
        return _finish(nc, ctx, plan, em, wait, d_out, stg, e_ps, _dbgpack())

    stats_region(x_own, xn_also=True)
    for i in range(NT):
        slot, wv = x_dma(x_own[i])
        wait("scalar", d_x[slot], wv)
        em("scalar", lambda e, i=i, s=slot: e.activation(
            xtmb[:, 0, :], acc[:, 1 + s, :], AF.Identity,
            bias=colbuf[:, i, NM1:NM1 + 1], scale=colbuf[:, i, RS1:RS1 + 1])
           .then_inc(s2v.h, 1))
        xfree.append([(s2v, s2v.inc())])
        wait("vector", s2v, s2v.n)
        em("vector", lambda e, i=i: e.tensor_scalar(
            xtmb[:, 1, :], xtmb[:, 0, :], colbuf[:, i, R2:R2 + 1], None, ALU.mult)
           .then_inc(stg.h, 1))
        stg.inc()
        wait("tensor", stg, stg.n)
        bank = i % 2
        tv = transpose_pack(bank, [xtmb[:, 0, c * P:(c + 1) * P] for c in range(DC)], DC)
        wait("vector", t2v, tv)
        em("vector", lambda e, i=i, b=bank: e.tensor_copy(
            xfm[:, :, i * P:(i + 1) * P], ptr[:, b, :, :]).then_inc(e_ps.h, 1))
        ptr_rel[bank] = (e_ps, e_ps.inc())
        bank2 = (i + 1) % 2
        tv2 = transpose_pack(bank2, [xtmb[:, 1, c * P:(c + 1) * P] for c in range(DC)], DC)
        wait("scalar", t2v, tv2)
        wait("vector", t2v, tv2)
        for c in range(DC):
            em("vector", lambda e, i=i, b=bank2, c=c: e.tensor_scalar(
                bufA[:, c, i * P:(i + 1) * P], ptr[:, b, c, :],
                gcols[:, c:c + 1], None, ALU.mult).then_inc(e_ps.h, 1))
            e_ps.inc()
        ptr_rel[bank2] = (e_ps, e_ps.n)
    P1A_DONE = e_ps.n

    wq_seq = [0]

    def wq_dma(src_ap):
        j = wq_seq[0]
        wq_seq[0] += 1
        s = j % 5
        if j - 5 in wq_cons:
            wait("gpsimd", t2v, wq_cons[j - 5])
        em("gpsimd", lambda e, s=s, a=src_ap: e.dma_start(out=wring[:, s, :, :], in_=a)
           .then_inc(d_wq[s].h, 16))
        return s, d_wq[s].dma()

    wq_cons = {}

    def qkva_mm(ws, wwv, rhs_fn, nslots=4):
        slots = acquire(nslots)
        wait("tensor", d_wq[ws], wwv)
        for dc in range(DC):
            for tg in range(nslots):
                last = dc == DC - 1 and tg == nslots - 1
                em("tensor", lambda e, ws=ws, dc=dc, s=slots[tg], r=rhs_fn(dc, tg),
                   st=(dc == 0), sp=(dc == DC - 1), l=last:
                   (e.matmul(pmm[:, s, :], wring[:, ws, dc, :], r, start=st, stop=sp)
                    .then_inc(t2v.h, 1) if l else
                    e.matmul(pmm[:, s, :], wring[:, ws, dc, :], r, start=st, stop=sp)))
        return slots, t2v.inc()

    carry_rhs = lambda dc, tg: s_tm[:, 2 * dc + tg // 2, (tg % 2) * 512:(tg % 2) * 512 + 512]
    wait("tensor", e_ps, P1C_VEC)
    for c in range(DC):
        ws, wwv = wq_dma(wq_d[3 * c])
        slots, tv = qkva_mm(ws, wwv, carry_rhs)
        wq_cons[wq_seq[0] - 1] = tv
        wait("vector", t2v, tv)
        for tg in range(4):
            em("vector", lambda e, tg=tg, s=slots[tg]: e.tensor_copy(
                kvb[:, tg * 512:(tg + 1) * 512], pmm[:, s, :]).then_inc(e_ps.h, 1))
            slot_rel[slots[tg]] = (e_ps, e_ps.inc())
        ws, wwv = wq_dma(wq_d[3 * c + 1])
        slots, tv = qkva_mm(ws, wwv, carry_rhs)
        wq_cons[wq_seq[0] - 1] = tv
        wait("scalar", t2v, tv)
        for tg in range(4):
            em("scalar", lambda e, tg=tg, s=slots[tg]: e.activation(
                ab[:, tg * 512:(tg + 1) * 512], pmm[:, s, :], AF.Sigmoid).then_inc(e_ps.h, 1))
            slot_rel[slots[tg]] = (e_ps, e_ps.inc())
        AEV = e_ps.n
        wait("vector", e_ps, AEV)
        em("vector", lambda e: e.tensor_tensor_scan(
            out=yb[:], data0=ab[:], data1=kvb[:], initial=0.0,
            op0=ALU.mult, op1=ALU.add).then_inc(s2v.h, 1))
        YC = s2v.inc()
        wait("scalar", s2v, YC)
        em("scalar", lambda e, c=c: e.activation(
            ics[c][:], yb[:, 2047:2048], AF.Identity, scale=flag[:, 0:1]).then_inc(s2v.h, 1))
        s2v.inc()
    IC_ALL = s2v.n
    if cap == "p2a":
        return _finish(nc, ctx, plan, em, wait, d_out, stg, e_ps, _dbgpack())

    own_rhs = lambda dc, tg: bufA[:, dc, tg * 512:tg * 512 + 512]
    wait("tensor", e_ps, P1A_DONE)

    def emit_s_transposes(c):
        wait("tensor", stg, stg.n)
        for half in range(2):
            bank = half
            tvs = transpose_pack(bank, [qb[:, (half * 8 + j) * P:(half * 8 + j + 1) * P]
                                        for j in range(8)], 8)
            wait("vector", t2v, tvs)
            em("vector", lambda e, c=c, h=half, b=bank: e.tensor_copy(
                s_tm[:, h * 8:(h + 1) * 8, c * P:(c + 1) * P], ptr[:, b, :, :])
               .then_inc(e_ps.h, 1))
            ptr_rel[bank] = (e_ps, e_ps.inc())

    for c in range(DC):
        ws, wwv = wq_dma(wq_d[3 * c])
        slots, tv = qkva_mm(ws, wwv, own_rhs)
        wq_cons[wq_seq[0] - 1] = tv
        wait("vector", t2v, tv)
        for tg in range(4):
            em("vector", lambda e, tg=tg, s=slots[tg]: e.tensor_copy(
                kvb[:, tg * 512:(tg + 1) * 512], pmm[:, s, :]).then_inc(e_ps.h, 1))
            slot_rel[slots[tg]] = (e_ps, e_ps.inc())
        ws, wwv = wq_dma(wq_d[3 * c + 1])
        slots, tv = qkva_mm(ws, wwv, own_rhs)
        wq_cons[wq_seq[0] - 1] = tv
        wait("scalar", t2v, tv)
        for tg in range(4):
            em("scalar", lambda e, tg=tg, s=slots[tg]: e.activation(
                ab[:, tg * 512:(tg + 1) * 512], pmm[:, s, :], AF.Sigmoid).then_inc(e_ps.h, 1))
            slot_rel[slots[tg]] = (e_ps, e_ps.inc())
        AEV = e_ps.n
        if c > 0:
            emit_s_transposes(c - 1)
        ws, wwv = wq_dma(wq_d[3 * c + 2])
        slots, tv = qkva_mm(ws, wwv, own_rhs)
        wq_cons[wq_seq[0] - 1] = tv
        wait("vector", t2v, tv)
        for tg in range(4):
            em("vector", lambda e, tg=tg, s=slots[tg]: e.tensor_copy(
                qb[:, tg * 512:(tg + 1) * 512], pmm[:, s, :]).then_inc(e_ps.h, 1))
            slot_rel[slots[tg]] = (e_ps, e_ps.inc())
        wait("vector", e_ps, AEV)
        wait("vector", s2v, IC_ALL)
        em("vector", lambda e, c=c: e.tensor_tensor_scan(
            out=yb[:], data0=ab[:], data1=kvb[:], initial=ics[c][:, 0:1],
            op0=ALU.mult, op1=ALU.add))
        em("vector", lambda e: e.tensor_tensor(qb[:], qb[:], yb[:], ALU.mult)
           .then_inc(stg.h, 1))
        stg.inc()
    rslots = acquire(1)
    RS = rslots[0]
    for j in range(NT):
        for dc in range(DC):
            last = j == NT - 1 and dc == DC - 1
            em("tensor", lambda e, j=j, dc=dc, s=RS, st=(dc == 0), sp=(dc == DC - 1), l=last:
               (e.matmul(pmm[:, s, j * NE:(j + 1) * NE], xfm[:, dc, j * P:(j + 1) * P],
                         wrsb[:, dc, :], start=st, stop=sp).then_inc(t2v.h, 1) if l else
                e.matmul(pmm[:, s, j * NE:(j + 1) * NE], xfm[:, dc, j * P:(j + 1) * P],
                         wrsb[:, dc, :], start=st, stop=sp)))
    RTV = t2v.inc()
    emit_s_transposes(DC - 1)
    lgP = lambda j: pmm[:, RS, j * NE:(j + 1) * NE]

    def ack_vs(v_count):
        wait("scalar", v2s, v_count)
        em("scalar", lambda e: e.nop().then_inc(s2v.h, 1))
        a = s2v.inc()
        wait("vector", s2v, a)

    wait("vector", t2v, RTV)
    for j in range(NT):
        em("vector", lambda e, j=j: e.reduce_max(colbuf[:, j, 0:1], lgP(j), AX.X))
    for j in range(NT):
        last = j == NT - 1
        em("vector", lambda e, j=j, l=last:
           (e.tensor_scalar(colbuf[:, j, 1:2], colbuf[:, j, 0:1], -1.0, None, ALU.mult)
            .then_inc(v2s.h, 1) if l else
            e.tensor_scalar(colbuf[:, j, 1:2], colbuf[:, j, 0:1], -1.0, None, ALU.mult)))
    ack_vs(v2s.inc())
    for j in range(NT):
        em("vector", lambda e, j=j: e.tensor_scalar(
            w8[:, j, :], lgP(j), colbuf[:, j, 0:1], None, ALU.is_ge))
    for j in range(NT):
        em("vector", lambda e, j=j: e.scalar_tensor_tensor(
            out=w8[:, j, :], in0=w8[:, j, :], scalar=-1e30, in1=lgP(j),
            op0=ALU.mult, op1=ALU.add))
    for j in range(NT):
        last = j == NT - 1
        em("vector", lambda e, j=j, l=last:
           (e.reduce_max(colbuf[:, j, 2:3], w8[:, j, :], AX.X).then_inc(v2s.h, 1)
            if l else e.reduce_max(colbuf[:, j, 2:3], w8[:, j, :], AX.X)))
    W8V = v2s.inc()
    wait("scalar", v2s, W8V)
    for j in range(NT):
        em("scalar", lambda e, j=j: e.activation(
            colbuf[:, j, 3:4], colbuf[:, j, 2:3], AF.Exp, bias=colbuf[:, j, 1:2]))
    for j in range(NT):
        last = j == NT - 1
        em("scalar", lambda e, j=j, l=last:
           (e.activation(gtmp[:, j * NE:(j + 1) * NE], lgP(j), AF.Exp,
                         bias=colbuf[:, j, 1:2]).then_inc(s2v.h, 1) if l else
            e.activation(gtmp[:, j * NE:(j + 1) * NE], lgP(j), AF.Exp,
                         bias=colbuf[:, j, 1:2])))
    W8S = s2v.inc()
    wait("vector", s2v, W8S)
    for j in range(NT):
        em("vector", lambda e, j=j: e.tensor_scalar(
            colbuf[:, j, 4:5], colbuf[:, j, 3:4], 1.0, None, ALU.add))
    for j in range(NT):
        last = j == NT - 1
        em("vector", lambda e, j=j, l=last:
           (e.reciprocal(colbuf[:, j, 5:6], colbuf[:, j, 4:5]).then_inc(v2s.h, 1)
            if l else e.reciprocal(colbuf[:, j, 5:6], colbuf[:, j, 4:5])))
    ack_vs(v2s.inc())
    for j in range(NT):
        em("vector", lambda e, j=j: e.tensor_scalar(
            w8[:, j, :], lgP(j), colbuf[:, j, 2:3], None, ALU.is_ge))
    for j in range(NT):
        last = j == NT - 1
        em("vector", lambda e, j=j, l=last:
           (e.scalar_tensor_tensor(
               out=w8[:, j, :], in0=gtmp[:, j * NE:(j + 1) * NE], scalar=colbuf[:, j, 5:6],
               in1=w8[:, j, :], op0=ALU.mult, op1=ALU.mult).then_inc(e_ps.h, 1) if l else
            e.scalar_tensor_tensor(
               out=w8[:, j, :], in0=gtmp[:, j * NE:(j + 1) * NE], scalar=colbuf[:, j, 5:6],
               in1=w8[:, j, :], op0=ALU.mult, op1=ALU.mult)))
    W8DONE_EPS = e_ps.inc()
    slot_rel[RS] = (e_ps, W8DONE_EPS)
    wait("scalar", e_ps, W8DONE_EPS)
    em("scalar", lambda e: e.nop().then_inc(s2v.h, 1))
    W8ACK = s2v.inc()
    wait("vector", s2v, W8ACK)

    if cap == "p25":
        return _finish(nc, ctx, plan, em, wait, d_out, stg, e_ps, _dbgpack())
    wg_slots = [0, 2, 4]
    wu_slots = [1, 3]
    W_LAST_T2V = t2v.n
    wgwu_seq = [0]
    gu_cons = {}
    ring_tenant = {0: None, 1: None, 2: None, 3: None, 4: None}

    def stream_chunk(src_ap, slot):
        k = wgwu_seq[0]
        wgwu_seq[0] += 1
        t = ring_tenant[slot]
        if t is not None:
            wait("gpsimd", t2v, gu_cons[t])
        else:
            wait("gpsimd", t2v, W_LAST_T2V)
        ring_tenant[slot] = k
        em("gpsimd", lambda e, s=slot, a=src_ap: e.dma_start(out=wring[:, s, :, :], in_=a)
           .then_inc(d_wq[s].h, 16))
        return d_wq[slot].dma()

    down_done_t2v = [0]

    def wd_stream_q(phase, e_, q):
        if q == 0 and phase > 0:
            wait("gpsimd", t2v, down_done_t2v[0])
        em("gpsimd", lambda e, e_=e_, q=q: e.dma_start(
            out=wd_sb[:, 4 * q:4 * q + 4, :], in_=wd_d[e_][:, 4 * q:4 * q + 4, :])
           .then_inc(d_wd.h, 16))
        d_wd.dma()
        return d_wd.n

    def wd_stream(phase, e_):
        for q in range(4):
            wd_stream_q(phase, e_, q)
        return d_wd.n

    wd_wait = [wd_stream(0, 0)]

    silu_rel = [None, None]
    acc_guard = [0] * 8
    deferred_fin = []

    n_phases = 16 if cap in ("full", "p3_noln") else int(cap.split("_")[1])
    for phase in range(n_phases):
        half = phase // 8
        e_ = phase % 8
        tok0 = half * 1024
        for fb in range(FC):
            if 2 <= fb <= 5 and phase > 0:
                wd_wait[0] = wd_stream_q(phase, e_, fb - 2)
            wgs = wg_slots[(phase * FC + fb) % 3]
            wus = wu_slots[(phase * FC + fb) % 2]
            wgv = stream_chunk(wg_d[e_, fb], wgs)
            wuv = stream_chunk(wu_d[e_, fb], wus)
            gslots = acquire(2)
            wait("tensor", d_wq[wgs], wgv)
            for dc in range(DC):
                for tg in range(2):
                    last = dc == DC - 1 and tg == 1
                    em("tensor", lambda e, ws=wgs, dc=dc, s=gslots[tg],
                       r=xfm[:, dc, tok0 + tg * 512: tok0 + tg * 512 + 512],
                       st=(dc == 0), sp=(dc == DC - 1), l=last:
                       (e.matmul(pmm[:, s, :], wring[:, ws, dc, :], r, start=st, stop=sp)
                        .then_inc(t2v.h, 1) if l else
                        e.matmul(pmm[:, s, :], wring[:, ws, dc, :], r, start=st, stop=sp)))
            GTV = t2v.inc()
            gu_cons[wgwu_seq[0] - 2] = GTV
            uslots = acquire(2)
            wait("tensor", d_wq[wus], wuv)
            for dc in range(DC):
                for tg in range(2):
                    last = dc == DC - 1 and tg == 1
                    em("tensor", lambda e, ws=wus, dc=dc, s=uslots[tg],
                       r=xfm[:, dc, tok0 + tg * 512: tok0 + tg * 512 + 512],
                       st=(dc == 0), sp=(dc == DC - 1), l=last:
                       (e.matmul(pmm[:, s, :], wring[:, ws, dc, :], r, start=st, stop=sp)
                        .then_inc(t2v.h, 1) if l else
                        e.matmul(pmm[:, s, :], wring[:, ws, dc, :], r, start=st, stop=sp)))
            UTV = t2v.inc()
            gu_cons[wgwu_seq[0] - 1] = UTV
            wait("scalar", t2v, GTV)
            if silu_rel[0] is not None:
                wait("scalar", silu_rel[0][0], silu_rel[0][1])
            em("scalar", lambda e, s=gslots[0]: e.activation(
                xtmb[:, 0, 0:512], pmm[:, s, :], AF.Silu).then_inc(e_ps.h, 1))
            slot_rel[gslots[0]] = (e_ps, e_ps.inc())
            if silu_rel[1] is not None:
                wait("scalar", silu_rel[1][0], silu_rel[1][1])
            em("scalar", lambda e, s=gslots[1]: e.activation(
                xtmb[:, 0, 512:1024], pmm[:, s, :], AF.Silu).then_inc(s2v.h, 1))
            SILU = s2v.inc()
            slot_rel[gslots[1]] = (s2v, SILU)
            wait("vector", s2v, SILU)
            wait("vector", t2v, UTV)
            for tg in range(2):
                hsl = bufA[:, fb // 2, (fb % 2) * 1024 + tg * 512:(fb % 2) * 1024 + tg * 512 + 512]
                em("vector", lambda e, s=uslots[tg], tg=tg, h=hsl: e.tensor_tensor(
                    h, xtmb[:, 0, tg * 512:(tg + 1) * 512], pmm[:, s, :], ALU.mult)
                   .then_inc(e_ps.h, 1))
                slot_rel[uslots[tg]] = (e_ps, e_ps.inc())
                silu_rel[tg] = (e_ps, e_ps.n)
        H_DONE = e_ps.n
        wait("tensor", d_wd, wd_wait[0])
        wait("tensor", e_ps, H_DONE)
        def finalize_pair(fhalf, tcs, acc_done):
            cbs = [fhalf * 8 + tc for tc in tcs]
            gws = [gtmp[:] if tc % 2 == 0 else ab[:, 1024:2048] for tc in tcs]
            wait("scalar", e_ps, acc_done)
            for tc, cb in zip(tcs, cbs):
                last = tc == tcs[-1]
                em("scalar", lambda e, tc=tc, cb=cb, l=last:
                   (e.activation(ab[:, 0:1024], acc[:, tc, :], AF.Identity,
                                 accum_out=colbuf[:, cb, SUM:SUM + 1]).then_inc(s2v.h, 1)
                    if l else
                    e.activation(ab[:, 0:1024], acc[:, tc, :], AF.Identity,
                                 accum_out=colbuf[:, cb, SUM:SUM + 1])))
            wait("vector", s2v, s2v.inc())
            for cb in cbs:
                last = cb == cbs[-1]
                em("vector", lambda e, cb=cb, l=last:
                   (e.tensor_scalar(colbuf[:, cb, NEGM:NEGM + 1], colbuf[:, cb, SUM:SUM + 1],
                                    -1.0 / D, None, ALU.mult).then_inc(v2s.h, 1) if l else
                    e.tensor_scalar(colbuf[:, cb, NEGM:NEGM + 1], colbuf[:, cb, SUM:SUM + 1],
                                    -1.0 / D, None, ALU.mult)))
            wait("scalar", v2s, v2s.inc())
            for tc, cb in zip(tcs, cbs):
                last = tc == tcs[-1]
                em("scalar", lambda e, tc=tc, cb=cb, l=last:
                   (e.activation(ab[:, 0:1024], acc[:, tc, :], AF.Square,
                                 bias=colbuf[:, cb, NEGM:NEGM + 1],
                                 accum_out=colbuf[:, cb, VS:VS + 1]).then_inc(s2v.h, 1)
                    if l else
                    e.activation(ab[:, 0:1024], acc[:, tc, :], AF.Square,
                                 bias=colbuf[:, cb, NEGM:NEGM + 1],
                                 accum_out=colbuf[:, cb, VS:VS + 1])))
            wait("vector", s2v, s2v.inc())
            for cb in cbs:
                last = cb == cbs[-1]
                em("vector", lambda e, cb=cb, l=last:
                   (e.tensor_scalar(colbuf[:, cb, V:V + 1], colbuf[:, cb, VS:VS + 1],
                                    1.0 / D, None, ALU.mult).then_inc(v2s.h, 1) if l else
                    e.tensor_scalar(colbuf[:, cb, V:V + 1], colbuf[:, cb, VS:VS + 1],
                                    1.0 / D, None, ALU.mult)))
            wait("scalar", v2s, v2s.inc())
            for cb in cbs:
                last = cb == cbs[-1]
                em("scalar", lambda e, cb=cb, l=last:
                   (e.activation(colbuf[:, cb, STD1:STD1 + 1], colbuf[:, cb, V:V + 1],
                                 AF.Sqrt, bias=epsc[:, 0:1]).then_inc(s2v.h, 1) if l else
                    e.activation(colbuf[:, cb, STD1:STD1 + 1], colbuf[:, cb, V:V + 1],
                                 AF.Sqrt, bias=epsc[:, 0:1])))
            wait("vector", s2v, s2v.inc())
            for cb in cbs:
                em("vector", lambda e, cb=cb: e.reciprocal(
                    colbuf[:, cb, RS1:RS1 + 1], colbuf[:, cb, STD1:STD1 + 1]))
            for cb in cbs:
                em("vector", lambda e, cb=cb: e.tensor_scalar(
                    colbuf[:, cb, SPARE:SPARE + 1], colbuf[:, cb, SUM:SUM + 1],
                    0.0, None, ALU.mult))
            for cb in cbs:
                last = cb == cbs[-1]
                em("vector", lambda e, cb=cb, l=last:
                   (e.tensor_tensor(colbuf[:, cb, NM1:NM1 + 1], colbuf[:, cb, NEGM:NEGM + 1],
                                    colbuf[:, cb, RS1:RS1 + 1], ALU.mult).then_inc(v2s.h, 1)
                    if l else
                    e.tensor_tensor(colbuf[:, cb, NM1:NM1 + 1], colbuf[:, cb, NEGM:NEGM + 1],
                                    colbuf[:, cb, RS1:RS1 + 1], ALU.mult)))
            wait("scalar", v2s, v2s.inc())
            for tc, cb in zip(tcs, cbs):
                last = tc == tcs[-1]
                em("scalar", lambda e, tc=tc, cb=cb, l=last:
                   (e.activation(acc[:, tc, :], acc[:, tc, :], AF.Identity,
                                 bias=colbuf[:, cb, NM1:NM1 + 1],
                                 scale=colbuf[:, cb, RS1:RS1 + 1]).then_inc(s2v.h, 1)
                    if l else
                    e.activation(acc[:, tc, :], acc[:, tc, :], AF.Identity,
                                 bias=colbuf[:, cb, NM1:NM1 + 1],
                                 scale=colbuf[:, cb, RS1:RS1 + 1])))
            wait("vector", s2v, s2v.inc())
            for tc, g in zip(tcs, gws):
                em("vector", lambda e, tc=tc, g=g: e.tensor_tensor(
                    g, acc[:, tc, :], lng[:], ALU.mult))
            for g in gws:
                last = g is gws[-1]
                em("vector", lambda e, g=g, l=last:
                   (e.tensor_tensor(g, g, lnb[:], ALU.add).then_inc(v2s.h, 1) if l else
                    e.tensor_tensor(g, g, lnb[:], ALU.add)))
            wait("scalar", v2s, v2s.inc())
            for g in gws:
                last = g is gws[-1]
                em("scalar", lambda e, g=g, l=last:
                   (e.activation(g, g, AF.Silu).then_inc(s2v.h, 1) if l else
                    e.activation(g, g, AF.Silu)))
            wait("vector", s2v, s2v.inc())
            for tc, cb, g in zip(tcs, cbs, gws):
                em("vector", lambda e, tc=tc, cb=cb, g=g: e.tensor_tensor(
                    acc[:, tc, :], s_tm[:, cb, :], g, ALU.mult).then_inc(v2g.h, 1))
                v2g.inc()
            wait("gpsimd", v2g, v2g.n)
            xa_cnt = []
            for tc, cb in zip(tcs, cbs):
                em("gpsimd", lambda e, tile=cb, tc=tc: e.dma_start(
                    out=acc[:, tc, :], in_=x_own[tile], accum_op=ALU.add)
                   .then_inc(d_xa.h, 16))
                xa_cnt.append(d_xa.dma())
            tail = fhalf == 1 and tcs[1] == 7
            for k, (tc, cb) in enumerate(zip(tcs, cbs)):
                q = "scalar" if (tail and k == 1) else "sync"
                wait(q, d_xa, xa_cnt[k])
                em(q, lambda e, tile=cb, tc=tc: e.dma_start(
                    out=out_d[tile], in_=acc[:, tc, :]).then_inc(d_out.h, 16))
                acc_guard[tc] = d_out.dma()

        for tc in range(8):
            dslots = acquire(2)
            for fc in range(FC):
                lhs = bufA[:, fc // 2, (fc % 2) * 1024 + tc * P:(fc % 2) * 1024 + (tc + 1) * P]
                for dh in range(2):
                    last = fc == FC - 1 and dh == 1
                    em("tensor", lambda e, l=lhs, fc=fc, s=dslots[dh], dh=dh,
                       st=(fc == 0), sp=(fc == FC - 1), l2=last:
                       (e.matmul(pmm[:, s, :], l, wd_sb[:, fc, dh * 512:(dh + 1) * 512],
                                 start=st, stop=sp).then_inc(t2v.h, 1) if l2 else
                        e.matmul(pmm[:, s, :], l, wd_sb[:, fc, dh * 512:(dh + 1) * 512],
                                 start=st, stop=sp)))
            DTV = t2v.inc()
            wait("vector", t2v, DTV)
            if e_ == 0:
                wait("vector", d_out, acc_guard[tc])
            for dh in range(2):
                aslice = acc[:, tc, dh * 512:(dh + 1) * 512]
                wcol = w8[:, half * 8 + tc, e_:e_ + 1]
                if e_ == 0:
                    em("vector", lambda e, s=dslots[dh], a=aslice, w=wcol: e.tensor_scalar(
                        a, pmm[:, s, :], w, None, ALU.mult).then_inc(e_ps.h, 1))
                else:
                    em("vector", lambda e, s=dslots[dh], a=aslice, w=wcol: e.scalar_tensor_tensor(
                        out=a, in0=pmm[:, s, :], scalar=w, in1=a,
                        op0=ALU.mult, op1=ALU.add).then_inc(e_ps.h, 1))
                slot_rel[dslots[dh]] = (e_ps, e_ps.inc())
            if e_ == 7 and tc % 2 == 1:
                finalize_pair(half, (tc - 1, tc), e_ps.n)
        down_done_t2v[0] = t2v.n

    return _finish(nc, ctx, plan, em, wait, d_out, stg, e_ps, _dbgpack())


def _finish(nc, ctx, plan, em, wait, d_out, stg, e_ps, dbg_pack=None):
    if dbg_pack is not None:
        dbg_o, srcs, d_dbg = dbg_pack
        for nm, src in srcs.items():
            em("sync", lambda e, o=dbg_o[nm], s=src: e.dma_start(out=o, in_=s)
               .then_inc(d_dbg.h, 16))
            d_dbg.dma()
        wait("sync", d_dbg, d_dbg.n)
    if d_out.n > 0:
        wait("sync", d_out, d_out.n)
    else:
        wait("sync", e_ps, e_ps.n)
        wait("sync", stg, stg.n)

    block = ctx.enter_context(nc.Block())

    @block.sync
    def _(e):
        for fn in plan["sync"]:
            fn(e)

    @block.gpsimd
    def _(e):
        for fn in plan["gpsimd"]:
            fn(e)

    @block.tensor
    def _(e):
        for fn in plan["tensor"]:
            fn(e)

    @block.vector
    def _(e):
        for fn in plan["vector"]:
            fn(e)

    @block.scalar
    def _(e):
        for fn in plan["scalar"]:
            fn(e)

    ctx.close()
    return nc


def _prep_weights(W_qkva, w_gate, w_up, w_down, w_router, gamma, ln_gamma, ln_beta):
    f32 = np.float32
    WQ = np.asarray(W_qkva, f32)
    A = WQ.reshape(24, P, DC, P).transpose(0, 3, 2, 1)
    order = []
    for c in range(DC):
        order += [8 + c, 16 + c, c]
    wq_l = np.ascontiguousarray(A[order])
    G = np.asarray(w_gate, f32).reshape(NE, FC, P, DC, P)
    wg_l = np.ascontiguousarray(G.transpose(0, 1, 4, 3, 2))
    U = np.asarray(w_up, f32).reshape(NE, FC, P, DC, P)
    wu_l = np.ascontiguousarray(U.transpose(0, 1, 4, 3, 2))
    wd_l = np.ascontiguousarray(np.asarray(w_down, f32).transpose(0, 2, 1)
                                .reshape(NE, FC, P, D).transpose(0, 2, 1, 3))
    wr_l = np.ascontiguousarray(np.asarray(w_router, f32).T.reshape(DC, P, NE)
                                .transpose(1, 0, 2))
    gc_l = np.ascontiguousarray(np.asarray(gamma, f32).reshape(DC, P).T)
    lng_l = np.ascontiguousarray(np.broadcast_to(np.asarray(ln_gamma, f32), (P, D)))
    lnb_l = np.ascontiguousarray(np.broadcast_to(np.asarray(ln_beta, f32), (P, D)))
    id_l = np.eye(P, dtype=ml_dtypes.bfloat16)
    return wq_l, wg_l, wu_l, wd_l, wr_l, gc_l, lng_l, lnb_l, id_l


def kernel(L0_attn_x, L0_attn_W_qkva, L0_attn_gamma, L0_ffn_w_router,
           L0_ffn_w_gate, L0_ffn_w_up, L0_ffn_w_down, L0_ffn_ln_gamma,
           L0_ffn_ln_beta):
    x = np.ascontiguousarray(np.asarray(L0_attn_x, np.float32))
    wq_l, wg_l, wu_l, wd_l, wr_l, gc_l, lng_l, lnb_l, id_l = _prep_weights(
        L0_attn_W_qkva, L0_ffn_w_gate, L0_ffn_w_up, L0_ffn_w_down,
        L0_ffn_w_router, L0_attn_gamma, L0_ffn_ln_gamma, L0_ffn_ln_beta)

    nc = build_program()
    in_maps = []
    for c in range(8):
        b, h = c // 2, c % 2
        xo = np.ascontiguousarray(x[b, h * 2048:(h + 1) * 2048].reshape(NT, P, D))
        xc = np.ascontiguousarray(x[b, 0:2048].reshape(NT, P, D)) if h == 1 else xo
        fl = np.full((P, 1), 1.0 if h == 1 else 0.0, np.float32)
        in_maps.append({
            "x_own": xo, "x_carry": xc, "flag_d": fl,
            "wq_d": wq_l, "wg_d": wg_l, "wu_d": wu_l, "wd_d": wd_l,
            "wr_d": wr_l, "gc_d": gc_l, "lng_d": lng_l, "lnb_d": lnb_l,
            "id_d": id_l,
        })
    res = run_bass_kernel_spmd(nc, in_maps, core_ids=list(range(8)))
    out = np.empty((4, 4096, 1024), np.float32)
    for c in range(8):
        b, h = c // 2, c % 2
        out[b, h * 2048:(h + 1) * 2048] = np.asarray(
            res.results[c]["out_d"], np.float32).reshape(2048, 1024)
    return out


if __name__ == "__main__":
    import reference
    inp = reference.setup_inputs(0)
    got = kernel(**{k: np.asarray(v) for k, v in inp.items()})
    ref = np.asarray(reference.reference(**inp))
    rvr = float(((got - ref) ** 2).mean() / (ref ** 2).mean())
    print("residual variance ratio:", rvr)
